# version 02
# baseline (speedup 1.0000x reference)
import numpy as np
import ml_dtypes
import concourse.bass as bass
import concourse.mybir as mybir
from concourse.bass_utils import run_bass_kernel_spmd

F32 = mybir.dt.float32
BF16 = mybir.dt.float16
I32 = mybir.dt.int32
AF = mybir.ActivationFunctionType
ALU = mybir.AluOpType
BF = np.float16

H = 4096
HP = 32
I = 11008
NI = 86
T = 2048
NTC = 4
TW = 512
NCORE = 8
CHG = [(0, 512), (512, 1920), (1920, 3328), (3328, 4736), (4736, 6144),
       (6144, 7552), (7552, 8960), (8960, 10368), (10368, 11008)]
CHD = [(0, 1408), (1408, 2816), (2816, 4096)]
CH_TILE_END = [e // 128 for (_, e) in CHG]


class Sch:

    def __init__(self, nc):
        self.nc = nc
        self.q = {"t": [], "v": [], "s": [], "y": [], "g": []}
        self.cnt = {}
        self.sems = {}

    def sem(self, name):
        if name not in self.sems:
            self.sems[name] = self.nc.alloc_semaphore(name)
            self.cnt[name] = 0
        return self.sems[name]

    def cur(self, name):
        self.sem(name)
        return self.cnt[name]

    def inc(self, name, amt):
        self.sem(name)
        self.cnt[name] += amt
        return self.cnt[name]

    def emit(self, qn, fn):
        self.q[qn].append(fn)


def build_nc(stage=99, phb_tc=NTC, phb_grp=8):
    nc = bass.Bass("TRN2", target_bir_lowering=False, detect_race_conditions=False)

    xT = nc.dram_tensor("xT", [H, T], F32, kind="ExternalInput").ap()
    qgT = nc.dram_tensor("qgT", [H, I], I32, kind="ExternalInput").ap()
    quT = nc.dram_tensor("quT", [H, I], I32, kind="ExternalInput").ap()
    qdT = nc.dram_tensor("qdT", [I, H], I32, kind="ExternalInput").ap()
    sgT = nc.dram_tensor("sgT", [32, I], F32, kind="ExternalInput").ap()
    zgT = nc.dram_tensor("zgT", [32, I], F32, kind="ExternalInput").ap()
    suT = nc.dram_tensor("suT", [32, I], F32, kind="ExternalInput").ap()
    zuT = nc.dram_tensor("zuT", [32, I], F32, kind="ExternalInput").ap()
    sdT = nc.dram_tensor("sdT", [NI, H], F32, kind="ExternalInput").ap()
    zdT = nc.dram_tensor("zdT", [NI, H], F32, kind="ExternalInput").ap()
    mask = nc.dram_tensor("mask", [H, H], I32, kind="ExternalInput").ap()
    wT = nc.dram_tensor("wT", [H, H], F32, kind="ExternalInput").ap()
    biasT = nc.dram_tensor("biasT", [128, 32], F32, kind="ExternalInput").ap()
    outT = nc.dram_tensor("outT", [H, T], F32, kind="ExternalOutput").ap()

    wg_s = nc.dram_tensor("wg_s", [H, I], BF16, kind="Internal").ap()
    wu_s = nc.dram_tensor("wu_s", [H, I], BF16, kind="Internal").ap()
    wd_s = nc.dram_tensor("wd_s", [I, H], BF16, kind="Internal").ap()
    sg_b = nc.dram_tensor("sg_b", [32, I], BF16, kind="Internal").ap()
    zg_b = nc.dram_tensor("zg_b", [32, I], BF16, kind="Internal").ap()
    su_b = nc.dram_tensor("su_b", [32, I], BF16, kind="Internal").ap()
    zu_b = nc.dram_tensor("zu_b", [32, I], BF16, kind="Internal").ap()
    sd_b = nc.dram_tensor("sd_b", [NI, H], BF16, kind="Internal").ap()
    zd_b = nc.dram_tensor("zd_b", [NI, H], BF16, kind="Internal").ap()
    hact_s = nc.dram_tensor("hact_s", [I, T], BF16, kind="Internal").ap()
    xcsp_s = nc.dram_tensor("xcsp_s", [H, T], BF16, kind="Internal").ap()
    xnew_s = nc.dram_tensor("xnew_s", [H, T], BF16, kind="Internal").ap()
    wm_s = nc.dram_tensor("wm_s", [H, H], BF16, kind="Internal").ap()
    bmh_s = nc.dram_tensor("bmh_s", [128, H], BF16, kind="Internal").ap()
    mrow_s = nc.dram_tensor("mrow_s", [2, T], F32, kind="Internal").ap()

    ones_col = nc.inline_tensor(np.ones((128, 1), dtype=BF), name="ones_col").ap()
    eps_c = nc.inline_tensor(np.full((128, 1), 1e-5, dtype=np.float32), name="eps_c").ap()
    negh_c = nc.inline_tensor(np.full((128, 1), -0.5, dtype=np.float32), name="negh_c").ap()
    b4_np = np.zeros((128, 4), dtype=np.float32)
    for n in range(128):
        b4_np[n, n // 32] = 1.0
    b4_c = nc.inline_tensor(b4_np, name="b4_c").ap()
    i128_c = nc.inline_tensor(np.eye(128, dtype=np.float32), name="i128_c").ap()
    e_np = np.zeros((128, H), dtype=BF)
    for n in range(H):
        e_np[n // 32, n] = 1.0
    e_c = nc.inline_tensor(e_np, name="e_c").ap()

    AR = nc.alloc_sbuf_tensor("AR", [128, HP * T], BF16)
    SLG = nc.alloc_sbuf_tensor("SLG", [128, 8192], BF16)
    SLU = nc.alloc_sbuf_tensor("SLU", [128, 8192], BF16)
    QS = nc.alloc_sbuf_tensor("QS", [128, 4096], BF16)
    SZB = nc.alloc_sbuf_tensor("SZB", [128, 5632], BF16)
    TMP = nc.alloc_sbuf_tensor("TMP", [128, 2048], BF16)
    GC = nc.alloc_sbuf_tensor("GC", [128, 4096], BF16)
    UC = nc.alloc_sbuf_tensor("UC", [128, 2048], BF16)
    RB = nc.alloc_sbuf_tensor("RB", [128, 2048], BF16)
    MB = nc.alloc_sbuf_tensor("MB", [128, 2048], BF16)
    BI = nc.alloc_sbuf_tensor("BI", [128, 32], F32)
    BMT = nc.alloc_sbuf_tensor("BMT", [128, 128], BF16)
    MRED = nc.alloc_sbuf_tensor("MRED", [128, 256], F32)
    SROW = nc.alloc_sbuf_tensor("SROW", [1, 512], F32)
    ONES = nc.alloc_sbuf_tensor("ONES", [128, 1], BF16)
    EPS = nc.alloc_sbuf_tensor("EPS", [128, 1], F32)
    NEGH = nc.alloc_sbuf_tensor("NEGH", [128, 1], F32)
    B4 = nc.alloc_sbuf_tensor("B4", [128, 4], F32)

    PSA = nc.alloc_psum_tensor("PSA", [128, 2048], F32)
    PSB = nc.alloc_psum_tensor("PSB", [128, 2048], F32)

    S = Sch(nc)

    def ar_x(hp, tc=None):
        if tc is None:
            return AR[:, hp * T:(hp + 1) * T]
        return AR[:, hp * T + tc * TW: hp * T + (tc + 1) * TW]

    def ar_h(it):
        return AR[:, it * TW:(it + 1) * TW]

    def ar_xc(hp):
        base = NI * TW
        return AR[:, base + hp * TW: base + (hp + 1) * TW]

    def pa_gu(tc):
        pg = PSA if tc % 2 == 0 else PSB
        off = 0 if tc < 2 else 1024
        return pg[:, off:off + 512], pg[:, off + 512:off + 1024]

    def emit_square(hp, wait_specs):
        S.inc("stq", 1)
        def f(e, hp=hp, ws=tuple(wait_specs), s=S.sem("stq")):
            for nm, val in ws:
                if val > 0:
                    e.wait_ge(S.sems[nm], val)
            e.activation(UC[:, :], ar_x(hp), AF.Square).then_inc(s, 1)
        S.emit("s", f)

    def emit_stats_mms(extra_wait):
        st0 = S.cur("stq") - HP
        for hp in range(HP):
            S.inc("stm", 1)
            def f(e, hp=hp, st0=st0, ew=tuple(extra_wait), s=S.sem("stm")):
                if hp == 0:
                    for nm, val in ew:
                        if val > 0:
                            e.wait_ge(S.sems[nm], val)
                e.wait_ge(S.sems["stq"], st0 + hp + 1)
                for tc in range(NTC):
                    e.matmul(PSA[0:1, tc * TW:(tc + 1) * TW], ONES[:, :],
                             ar_x(hp, tc), start=(hp == 0), stop=(hp == HP - 1))
                last = None
                for tc in range(NTC):
                    last = e.matmul(PSB[0:1, tc * TW:(tc + 1) * TW], ONES[:, :],
                                    UC[:, tc * TW:(tc + 1) * TW],
                                    start=(hp == 0), stop=(hp == HP - 1))
                last.then_inc(s, 1)
            S.emit("t", f)

    def emit_stats_tail(sub):
        stm_n = S.cur("stm")
        bb_prev = S.cur("bb")
        MRW0 = S.cur("mrw")
        for tc in range(NTC):
            strA = S.inc("str", 1)
            S.inc("mrw", 16)
            mrwA = S.cur("mrw")
            S.inc("srr", 1)
            srrA = S.cur("srr")
            S.inc("str", 1)
            strB = S.cur("str")
            S.inc("mrw", 16)
            last_tc = tc == NTC - 1
            if last_tc:
                S.inc("ste", 1)
            def fv1(e, tc=tc, stm_n=stm_n, strA=strA, mrwA=mrwA, MRW0=MRW0,
                    s_str=S.sem("str"), s_srr=S.sem("srr")):
                if tc == 0:
                    e.wait_ge(S.sems["stm"], stm_n)
                else:
                    e.wait_ge(S.sems["mrw"], MRW0 + 32 * tc)
                e.tensor_single_scalar(SROW[0:1, :], PSA[0:1, tc * TW:(tc + 1) * TW],
                                       1.0 / H, ALU.mult).then_inc(s_str, 1)
                e.wait_ge(S.sems["mrw"], mrwA)
                e.tensor_mul(SROW[0:1, :], SROW[0:1, :], SROW[0:1, :])
                ins = e.scalar_tensor_tensor(SROW[0:1, :], PSB[0:1, tc * TW:(tc + 1) * TW],
                                             1.0 / H, SROW[0:1, :], ALU.mult, ALU.subtract)
                ins.then_inc(s_srr, 1)
            S.emit("v", fv1)
            def fs1(e, tc=tc, strA=strA, srrA=srrA, strB=strB, bb_prev=bb_prev,
                    last_tc=last_tc, s_mrw=S.sem("mrw"), s_str=S.sem("str"),
                    s_ste=S.sem("ste")):
                e.wait_ge(S.sems["str"], strA)
                if bb_prev > 0 and tc == 0:
                    e.wait_ge(S.sems["bb"], bb_prev)
                e.dma_start(out=mrow_s[0:1, tc * TW:(tc + 1) * TW],
                            in_=SROW[0:1, :]).then_inc(s_mrw, 16)
                e.wait_ge(S.sems["srr"], srrA)
                if last_tc:
                    e.sem_inc(s_ste, 1)
                e.activation(SROW[0:1, :], SROW[0:1, :], AF.Ln, bias=EPS[0:1, :])
                e.activation(SROW[0:1, :], SROW[0:1, :], AF.Exp,
                             scale=NEGH[0:1, :]).then_inc(s_str, 1)
                e.wait_ge(S.sems["str"], strB)
                e.dma_start(out=mrow_s[1:2, tc * TW:(tc + 1) * TW],
                            in_=SROW[0:1, :]).then_inc(s_mrw, 16)
            S.emit("s", fs1)
        mrw_n = S.cur("mrw")
        consumer_waits = []
        if sub == 2:
            consumer_waits = [("ev", S.cur("ev")), ("yev", S.cur("yev")), ("ctr", S.cur("ctr"))]
        elif sub == 3:
            consumer_waits = [("aev", S.cur("aev")), ("ctr", S.cur("ctr"))]
        S.inc("bb", 32)
        def fg(e, mrw_n=mrw_n, cw=tuple(consumer_waits), s=S.sem("bb")):
            e.wait_ge(S.sems["mrw"], mrw_n)
            for nm, val in cw:
                if val > 0:
                    e.wait_ge(S.sems[nm], val)
            e.dma_start(out=MB[:, :], in_=mrow_s[0:1, :].broadcast_to([128, T])).then_inc(s, 16)
            e.dma_start(out=RB[:, :], in_=mrow_s[1:2, :].broadcast_to([128, T])).then_inc(s, 16)
        S.emit("g", fg)
        bb_n = S.cur("bb")
        for hp in range(HP):
            S.inc("ctr", 1)
            def fc(e, hp=hp, bb_n=bb_n, stm_n=stm_n, s=S.sem("ctr")):
                if hp == 0:
                    e.wait_ge(S.sems["bb"], bb_n)
                    e.wait_ge(S.sems["stm"], stm_n)
                e.tensor_sub(ar_x(hp), ar_x(hp), MB[:, :]).then_inc(s, 1)
            S.emit("v", fc)

    def emit_stats_full(sub, ready_waits, psum_free_waits):
        stq0 = S.cur("stq")
        stm0 = S.cur("stm")
        for hp in range(HP):
            ws = list(ready_waits) if hp == 0 else []
            if hp >= 1:
                ws.append(("stm", stm0 + hp))
            emit_square(hp, ws)
        emit_stats_mms(psum_free_waits)
        emit_stats_tail(sub)

    dq_state = {"count": 0, "slot_writer": [None, None]}

    def emit_dq_iter(qsrc, s_bf, z_bf, wdst, grp, c0, c1, wbase):
        it_par = dq_state["count"] % 2
        dq_state["count"] += 1
        csz = c1 - c0
        qs_v = QS[:, it_par * 2048: it_par * 2048 + csz]
        sb_v = SZB[:, it_par * 2816: it_par * 2816 + csz]
        zb_v = SZB[:, it_par * 2816 + 1408: it_par * 2816 + 1408 + csz]
        n_dqv_gate = S.cur("dqv") - 2 if dq_state["count"] >= 2 else 0
        prev_writer = dq_state["slot_writer"][it_par]
        qsem = "qld%d" % it_par
        ssem = "bls%d" % it_par
        zsem = "blz%d" % it_par
        wsem = wbase + str(it_par)
        S.inc(qsem, 16)
        def fq(e, qsrc=qsrc, grp=grp, c0=c0, c1=c1, qs_v=qs_v, pw=prev_writer,
               s=S.sem(qsem)):
            if pw is not None:
                e.wait_ge(S.sems[pw[0]], pw[1])
            e.dma_start(out=qs_v, in_=qsrc[grp * 128:(grp + 1) * 128, c0:c1]).then_inc(s, 16)
        S.emit("g", fq)
        S.inc(ssem, 16)
        def fbs(e, s_bf=s_bf, grp=grp, c0=c0, c1=c1, sb_v=sb_v, nv=n_dqv_gate,
                s=S.sem(ssem)):
            if nv > 0:
                e.wait_ge(S.sems["dqv"], nv)
            e.dma_start(out=sb_v, in_=s_bf[grp:grp + 1, c0:c1].broadcast_to([128, c1 - c0])).then_inc(s, 16)
        S.emit("s", fbs)
        S.inc(zsem, 16)
        def fbz(e, z_bf=z_bf, grp=grp, c0=c0, c1=c1, zb_v=zb_v, nv=n_dqv_gate,
                s=S.sem(zsem)):
            if nv > 0:
                e.wait_ge(S.sems["dqv"], nv)
            e.dma_start(out=zb_v, in_=z_bf[grp:grp + 1, c0:c1].broadcast_to([128, c1 - c0])).then_inc(s, 16)
        S.emit("y", fbz)
        S.inc("dqv", 2)
        nq = S.cur(qsem)
        ns = S.cur(ssem)
        nz = S.cur(zsem)
        def fv(e, qs_v=qs_v, sb_v=sb_v, zb_v=zb_v, nq=nq, ns=ns, nz=nz, qsem=qsem,
               ssem=ssem, zsem=zsem, s=S.sem("dqv")):
            e.wait_ge(S.sems[qsem], nq)
            e.wait_ge(S.sems[ssem], ns)
            e.wait_ge(S.sems[zsem], nz)
            e.tensor_sub(qs_v, qs_v, zb_v).then_inc(s, 1)
            e.tensor_mul(qs_v, qs_v, sb_v).then_inc(s, 1)
        S.emit("v", fv)
        nv2 = S.cur("dqv")
        S.inc(wsem, 16)
        dq_state["slot_writer"][it_par] = (wsem, S.cur(wsem))
        def fw(e, wdst=wdst, grp=grp, c0=c0, c1=c1, qs_v=qs_v, nv=nv2, s=S.sem(wsem)):
            e.wait_ge(S.sems["dqv"], nv)
            e.dma_start(out=wdst[grp * 128:(grp + 1) * 128, c0:c1], in_=qs_v).then_inc(s, 16)
        S.emit("y", fw)

    def emit_dq_chunk_gu(ci):
        c0, c1 = CHG[ci]
        for qsrc, s_bf, z_bf, wdst in ((qgT, sg_b, zg_b, wg_s), (quT, su_b, zu_b, wu_s)):
            for grp in range(32):
                emit_dq_iter(qsrc, s_bf, z_bf, wdst, grp, c0, c1, "dqw")

    def emit_dq_down():
        for gi in range(NI):
            for (c0, c1) in CHD:
                emit_dq_iter(qdT, sd_b, zd_b, wd_s, gi, c0, c1, "dqd")

    def emit_phaseA_tile(k, p, bases, extra_mm_waits=(), extra_ev_waits=(),
                         extra_sl_waits=()):
        MM0, EV0, HS0, HV0, HA0 = bases
        buf = k % 2
        wg3 = wg_s.rearrange("(hp p) i -> p hp i", p=128)
        wu3 = wu_s.rearrange("(hp p) i -> p hp i", p=128)
        dqw_w = [("dqw0", S.cur("dqw0")), ("dqw1", S.cur("dqw1"))] if p == 1 else []
        mm_n = MM0 + 4 * (k - 1) if k >= 2 else 0
        slsem = "sl%d" % buf
        for src3, dstT in ((wg3, SLG), (wu3, SLU)):
            S.inc(slsem, 16)
            def fl(e, k=k, buf=buf, src3=src3, dstT=dstT, dqw_w=tuple(dqw_w), mm_n=mm_n,
                   esw=tuple(extra_sl_waits), s=S.sem(slsem)):
                for nm, val in esw:
                    if val > 0:
                        e.wait_ge(S.sems[nm], val)
                for nm, val in dqw_w:
                    if val > 0:
                        e.wait_ge(S.sems[nm], val)
                if mm_n:
                    e.wait_ge(S.sems["mm"], mm_n)
                dst = dstT[:, buf * 4096:(buf + 1) * 4096].rearrange("p (hp i) -> p hp i", hp=HP)
                e.dma_start(out=dst, in_=src3[:, :, k * 128:(k + 1) * 128]).then_inc(s, 16)
            S.emit("y", fl)
        sl_n = S.cur(slsem)
        for tc in range(NTC):
            S.inc("mm", 1)
            def ft(e, k=k, tc=tc, buf=buf, sl_n=sl_n, EV0=EV0, emw=tuple(extra_mm_waits),
                   s=S.sem("mm")):
                if tc == 0:
                    e.wait_ge(S.sems["sl%d" % buf], sl_n)
                    if k == 0:
                        for nm, val in emw:
                            if val > 0:
                                e.wait_ge(S.sems[nm], val)
                if k > 0:
                    e.wait_ge(S.sems["ev"], EV0 + 8 * (k - 1) + 2 * tc + 2)
                gs, us = pa_gu(tc)
                for hp in range(HP):
                    lg = SLG[:, buf * 4096 + hp * 128: buf * 4096 + (hp + 1) * 128]
                    lu = SLU[:, buf * 4096 + hp * 128: buf * 4096 + (hp + 1) * 128]
                    rx = ar_x(hp, tc)
                    e.matmul(gs, lg, rx, start=(hp == 0), stop=(hp == HP - 1))
                    mmi = e.matmul(us, lu, rx, start=(hp == 0), stop=(hp == HP - 1))
                    if hp == HP - 1:
                        mmi.then_inc(s, 1)
            S.emit("t", ft)
        hs_list = []
        for tc in range(NTC):
            S.inc("ev", 2)
        for tc in range(NTC):
            S.inc("hs", 1)
            hs_list.append(S.cur("hs"))
        for tc in range(NTC):
            S.inc("hv", 1)
        def fv(e, k=k, p=p, buf=buf, MM0=MM0, HA0=HA0, hs_list=tuple(hs_list),
               eew=tuple(extra_ev_waits), s_ev=S.sem("ev"), s_hv=S.sem("hv")):
            for tc in range(NTC):
                e.wait_ge(S.sems["mm"], MM0 + 4 * k + tc + 1)
                if tc == 0:
                    if k >= 2:
                        e.wait_ge(S.sems["ha%d" % buf], HA0[buf] + 16 * ((k - 2) // 2 + 1))
                    if k == 0:
                        for nm, val in eew:
                            if val > 0:
                                e.wait_ge(S.sems[nm], val)
                gs, us = pa_gu(tc)
                gslot = GC[:, buf * 2048 + tc * TW: buf * 2048 + (tc + 1) * TW]
                e.tensor_mul(gslot, gs, RB[:, tc * TW:(tc + 1) * TW]).then_inc(s_ev, 1)
                e.tensor_mul(UC[:, tc * TW:(tc + 1) * TW], us,
                             RB[:, tc * TW:(tc + 1) * TW]).then_inc(s_ev, 1)
            for tc in range(NTC):
                e.wait_ge(S.sems["hs"], hs_list[tc])
                gslot = GC[:, buf * 2048 + tc * TW: buf * 2048 + (tc + 1) * TW]
                e.tensor_mul(gslot, gslot, UC[:, tc * TW:(tc + 1) * TW]).then_inc(s_hv, 1)
        S.emit("v", fv)
        S.inc("ha%d" % buf, 16)
        def fsc(e, k=k, buf=buf, EV0=EV0, HV0=HV0, s_hs=S.sem("hs"), s_ha=S.sem("ha%d" % buf)):
            for tc in range(NTC):
                e.wait_ge(S.sems["ev"], EV0 + 8 * k + 2 * tc + 1)
                gslot = GC[:, buf * 2048 + tc * TW: buf * 2048 + (tc + 1) * TW]
                e.activation(gslot, gslot, AF.Silu).then_inc(s_hs, 1)
            e.wait_ge(S.sems["hv"], HV0 + 4 * k + 4)
            e.dma_start(out=hact_s[k * 128:(k + 1) * 128, :],
                        in_=GC[:, buf * 2048:(buf + 1) * 2048]).then_inc(s_ha, 16)
        S.emit("s", fsc)

    def emit_xc_spill(p):
        ctr_n = S.cur("ctr")
        xcr_prev = S.cur("xcr")
        for hp in range(HP):
            S.inc("xcw", 16)
            def f(e, hp=hp, ctr_n=ctr_n, xcr_prev=xcr_prev, s=S.sem("xcw")):
                if hp == 0:
                    e.wait_ge(S.sems["ctr"], ctr_n)
                    if xcr_prev > 0:
                        e.wait_ge(S.sems["xcr"], xcr_prev)
                e.dma_start(out=xcsp_s[hp * 128:(hp + 1) * 128, :], in_=ar_x(hp)).then_inc(s, 16)
            S.emit("s", f)

    def emit_phaseB(p, n_tc=NTC, n_grp=8):
        MM_all = S.cur("mm")
        XCW_all = S.cur("xcw")
        HA_all = (S.cur("ha0"), S.cur("ha1"))
        DQD_all = (S.cur("dqd0"), S.cur("dqd1"))
        HCL0 = S.cur("hcl")
        XCR0 = S.cur("xcr")

        YIT0 = S.cur("yit")

        YEV0 = S.cur("yev")
        wsem = "xnw" if p == 1 else "osw"
        XNW0 = S.cur(wsem)
        wdsem = "wdl%d" % p
        WDC0 = S.cur("wdc%d" % p)
        EV_all = S.cur("ev")
        AEV_all = S.cur("aev")
        STE_all = S.cur("ste")

        def out_slot(par, j, p):
            if p == 1:
                return GC[:, par * 2048 + j * 512: par * 2048 + (j + 1) * 512]
            if par == 0:
                return GC[:, :].bitcast(F32)[:, j * 512:(j + 1) * 512]
            if j < 2:
                return UC[:, :].bitcast(F32)[:, j * 512:(j + 1) * 512]
            return TMP[:, :].bitcast(F32)[:, (j - 2) * 512:(j - 1) * 512]

        for tc in range(n_tc):
            yit_w = YIT0 + 88 * tc
            yev_w = YEV0 + 32 * tc
            for it in range(NI):
                S.inc("hcl", 16)
                def fh(e, it=it, tc=tc, yit_w=yit_w, yev_w=yev_w, MM_all=MM_all,
                       XCW_all=XCW_all, HA_all=HA_all, s=S.sem("hcl")):
                    if it == 0:
                        if tc == 0:
                            e.wait_ge(S.sems["mm"], MM_all)
                            e.wait_ge(S.sems["xcw"], XCW_all)
                            e.wait_ge(S.sems["ha0"], HA_all[0])
                            e.wait_ge(S.sems["ha1"], HA_all[1])
                        else:
                            e.wait_ge(S.sems["yit"], yit_w)
                            e.wait_ge(S.sems["yev"], yev_w)
                    e.dma_start(out=ar_h(it),
                                in_=hact_s[it * 128:(it + 1) * 128, tc * TW:(tc + 1) * TW]).then_inc(s, 16)
                S.emit("g", fh)
            for hp in range(HP):
                S.inc("xcr", 16)
                def fx(e, hp=hp, tc=tc, s=S.sem("xcr")):
                    e.dma_start(out=ar_xc(hp),
                                in_=xcsp_s[hp * 128:(hp + 1) * 128, tc * TW:(tc + 1) * TW]).then_inc(s, 16)
                S.emit("g", fx)
            hcl_n = S.cur("hcl")
            xcr_n = S.cur("xcr")

            for grp in range(n_grp):
                gidx = tc * 8 + grp
                for it in range(NI):
                    ridx = gidx * NI + it
                    slot = ridx % 16
                    S.inc(wdsem, 16)
                    def fw(e, it=it, grp=grp, slot=slot, ridx=ridx, p=p,
                           YIT0=YIT0, DQD_all=DQD_all, MM_all=MM_all, wdsem=wdsem,
                           s=S.sem(wdsem)):
                        if ridx == 0:
                            if p == 1:
                                e.wait_ge(S.sems["dqd0"], DQD_all[0])
                                e.wait_ge(S.sems["dqd1"], DQD_all[1])
                            e.wait_ge(S.sems["mm"], MM_all)
                        if ridx >= 16:
                            g2, it2 = divmod(ridx - 16, NI)
                            e.wait_ge(S.sems["yit"], YIT0 + g2 * 11 + it2 // 8 + 1)
                        e.dma_start(out=SLG[:, slot * 512:(slot + 1) * 512],
                                    in_=wd_s[it * 128:(it + 1) * 128,
                                             grp * 512:(grp + 1) * 512]).then_inc(s, 16)
                    S.emit("y", fw)
                    if it % 8 == 7 or it == NI - 1:
                        cl_end = gidx * NI + min((it // 8 + 1) * 8, NI)
                        S.inc("wdc%d" % p, 1)
                        def fm(e, n16=16 * cl_end, wdsem=wdsem,
                               s=S.sem("wdc%d" % p)):
                            e.wait_ge(S.sems[wdsem], n16)
                            e.sem_inc(s, 1)
                        S.emit("y", fm)
                for it in range(NI):
                    if it % 8 == 7 or it == NI - 1:
                        S.inc("yit", 1)
                def ft(e, tc=tc, grp=grp, gidx=gidx, p=p, hcl_n=hcl_n,
                       wdsem=wdsem, WDC0=WDC0, YEV0=YEV0, EV_all=EV_all, AEV_all=AEV_all,
                       STE_all=STE_all, s_yit=S.sem("yit")):
                    pg = PSA if gidx % 2 == 0 else PSB
                    if grp == 0:
                        e.wait_ge(S.sems["hcl"], hcl_n)
                    if gidx < 2:
                        if p == 1:
                            e.wait_ge(S.sems["ev"], EV_all)
                        else:
                            e.wait_ge(S.sems["aev"], AEV_all)
                        e.wait_ge(S.sems["ste"], STE_all)
                    else:
                        e.wait_ge(S.sems["yev"], YEV0 + 4 * (gidx - 1))
                    for it in range(NI):
                        ridx = gidx * NI + it
                        if it % 8 == 0:
                            e.wait_ge(S.sems["wdc%d" % p], WDC0 + gidx * 11 + it // 8 + 1)
                        slot = ridx % 16
                        last = None
                        for j in range(4):
                            last = e.matmul(pg[:, j * 512:(j + 1) * 512],
                                            SLG[:, slot * 512 + j * 128: slot * 512 + (j + 1) * 128],
                                            ar_h(it), start=(it == 0), stop=(it == NI - 1))
                        if it % 8 == 7 or it == NI - 1:
                            last.then_inc(s_yit, 1)
                S.emit("t", ft)
                yit_n = S.cur("yit")
                for j in range(4):
                    S.inc("yev", 1)
                yev_after = S.cur("yev")
                def fv(e, tc=tc, grp=grp, gidx=gidx, p=p, yit_n=yit_n, xcr_n=xcr_n,
                       HA_all=HA_all, XNW0=XNW0, wsem=wsem, s=S.sem("yev")):
                    pg = PSA if gidx % 2 == 0 else PSB
                    e.wait_ge(S.sems["yit"], yit_n)
                    if gidx == 0:
                        e.wait_ge(S.sems["xcr"], xcr_n)
                        e.wait_ge(S.sems["ha0"], HA_all[0])
                        e.wait_ge(S.sems["ha1"], HA_all[1])
                    elif grp == 0:
                        e.wait_ge(S.sems["xcr"], xcr_n)
                    if gidx >= 2:
                        e.wait_ge(S.sems[wsem], XNW0 + 16 * 4 * (gidx - 1))
                    for j in range(4):
                        hp = grp * 4 + j
                        slot = out_slot(gidx % 2, j, p)
                        e.scalar_tensor_tensor(slot, pg[:, j * 512:(j + 1) * 512], 0.5,
                                               ar_xc(hp), ALU.mult, ALU.add)
                        e.tensor_add(slot, slot, MB[:, tc * TW:(tc + 1) * TW]).then_inc(s, 1)
                S.emit("v", fv)
                for j in range(4):
                    hp = grp * 4 + j
                    S.inc(wsem, 16)
                    def fst(e, tc=tc, gidx=gidx, j=j, hp=hp, p=p, yev_after=yev_after,
                            s=S.sem(wsem)):
                        e.wait_ge(S.sems["yev"], yev_after - 3 + j)
                        slot = out_slot(gidx % 2, j, p)
                        if p == 1:
                            dst = xnew_s[hp * 128:(hp + 1) * 128, tc * TW:(tc + 1) * TW]
                        else:
                            dst = outT[hp * 128:(hp + 1) * 128, tc * TW:(tc + 1) * TW]
                        e.dma_start(out=dst, in_=slot).then_inc(s, 16)
                    S.emit("s", fst)

    def emit_wm_build():
        dq_all = [(nm, S.cur(nm)) for nm in ("dqw0", "dqw1", "dqd0", "dqd1")]
        bhw_all = S.cur("bhw")
        dqv_all = S.cur("dqv")
        for kp in range(HP):
            S.inc("wtl", 16)
            def fg(e, kp=kp, dq_all=tuple(dq_all), dqv_all=dqv_all, s=S.sem("wtl")):
                if kp == 0:
                    for nm, val in dq_all:
                        if val > 0:
                            e.wait_ge(S.sems[nm], val)
                    e.wait_ge(S.sems["dqv"], dqv_all)
                else:
                    e.wait_ge(S.sems["wmv"], 2 * kp)
                e.dma_start(out=QS[:, :], in_=wT[kp * 128:(kp + 1) * 128, :]).then_inc(s, 16)
            S.emit("g", fg)
            S.inc("wte", 16 * 4)
            def fge(e, kp=kp, bhw_all=bhw_all, dqv_all=dqv_all, s=S.sem("wte")):
                if kp == 0:
                    e.wait_ge(S.sems["bhw"], bhw_all)
                    e.wait_ge(S.sems["dqv"], dqv_all)
                else:
                    e.wait_ge(S.sems["wmv"], 2 * kp)
                for j in range(4):
                    kb = kp * 4 + j
                    e.dma_start(out=SZB[32 * j:32 * (j + 1), 0:4096],
                                in_=bmh_s[kb:kb + 1, :].broadcast_to([32, 4096])).then_inc(s, 16)
            S.emit("s", fge)
            wtl_n = S.cur("wtl")
            wte_n = S.cur("wte")
            for h in range(2):
                S.inc("wmv", 1)
                wmb_need = 16 * (2 * kp + h)
                def fv(e, kp=kp, h=h, wtl_n=wtl_n, wte_n=wte_n, wmb_need=wmb_need,
                       s=S.sem("wmv")):
                    if h == 0:
                        e.wait_ge(S.sems["wtl"], wtl_n)
                        e.wait_ge(S.sems["wte"], wte_n)
                    if wmb_need > 0:
                        e.wait_ge(S.sems["wmb"], wmb_need)
                    e.tensor_mul(TMP[:, :], QS[:, h * 2048:(h + 1) * 2048],
                                 SZB[:, h * 2048:(h + 1) * 2048]).then_inc(s, 1)
                S.emit("v", fv)
                wmv_n = S.cur("wmv")
                S.inc("wmb", 16)
                def fs2(e, kp=kp, h=h, wmv_n=wmv_n, s=S.sem("wmb")):
                    e.wait_ge(S.sems["wmv"], wmv_n)
                    e.dma_start(out=wm_s[kp * 128:(kp + 1) * 128, h * 2048:(h + 1) * 2048],
                                in_=TMP[:, :]).then_inc(s, 16)
                S.emit("s", fs2)

    def emit_attn():
        XRL_all = S.cur("xrl")
        WMB_all = S.cur("wmb")
        YIT_all = S.cur("yit")
        YEV_all = S.cur("yev")
        BB_all = S.cur("bb")
        CTR_all = S.cur("ctr")
        AWL0 = S.cur("awl")
        AWC0 = S.cur("awc")
        AIT0 = S.cur("ait")
        AEV0 = S.cur("aev")
        XNW_all = S.cur("xnw")
        STE_all = S.cur("ste")

        def ctmp(kp):
            tens = SLG if kp < 16 else SLU
            return tens[:, (kp % 16) * 512:(kp % 16 + 1) * 512]

        for tc in range(NTC):
            ait_w = AIT0 + 64 * tc
            for kp in range(HP):
                S.inc("atc", 1)
            atc_n = S.cur("atc")
            def fcp(e, tc=tc, ait_w=ait_w, XRL_all=XRL_all, YIT_all=YIT_all,
                    CTR_all=CTR_all, YEV_all=YEV_all, s=S.sem("atc")):
                if tc == 0:
                    e.wait_ge(S.sems["xrl"], XRL_all)
                    e.wait_ge(S.sems["yit"], YIT_all)
                    e.wait_ge(S.sems["yev"], YEV_all)
                    e.wait_ge(S.sems["ctr"], CTR_all)
                else:
                    e.wait_ge(S.sems["ait"], ait_w)
                for kp in range(HP):
                    e.tensor_copy(ctmp(kp), ar_x(kp, tc)).then_inc(s, 1)
            S.emit("v", fcp)

            for grp in range(8):
                gidx = tc * 8 + grp
                for kp in range(HP):
                    ridx = gidx * HP + kp
                    slot = ridx % 11
                    S.inc("awl", 16)
                    def fw(e, kp=kp, grp=grp, slot=slot, ridx=ridx, WMB_all=WMB_all,
                           AIT0=AIT0, s=S.sem("awl")):
                        if ridx == 0:
                            e.wait_ge(S.sems["wmb"], WMB_all)
                            e.wait_ge(S.sems["wmv"], 2 * HP)
                        if ridx >= 11:
                            g2, kp2 = divmod(ridx - 11, HP)
                            e.wait_ge(S.sems["ait"], AIT0 + g2 * 8 + kp2 // 4 + 1)
                        e.dma_start(out=SZB[:, slot * 512:(slot + 1) * 512],
                                    in_=wm_s[kp * 128:(kp + 1) * 128,
                                             grp * 512:(grp + 1) * 512]).then_inc(s, 16)
                    S.emit("y", fw)
                    if kp % 4 == 3:
                        cl_end = gidx * HP + (kp // 4 + 1) * 4
                        S.inc("awc", 1)
                        def fm(e, n16=16 * cl_end, s=S.sem("awc")):
                            e.wait_ge(S.sems["awl"], n16)
                            e.sem_inc(s, 1)
                        S.emit("y", fm)
                for kp in range(HP):
                    if kp % 4 == 3:
                        S.inc("ait", 1)
                def ft(e, tc=tc, grp=grp, gidx=gidx, atc_n=atc_n, AWC0=AWC0,
                       AEV0=AEV0, YEV_all=YEV_all, STE_all=STE_all,
                       s_ait=S.sem("ait")):
                    pg = PSA if gidx % 2 == 0 else PSB
                    if grp == 0:
                        e.wait_ge(S.sems["atc"], atc_n)
                    if gidx < 2:
                        e.wait_ge(S.sems["yev"], YEV_all)
                        e.wait_ge(S.sems["ste"], STE_all)
                    else:
                        e.wait_ge(S.sems["aev"], AEV0 + 4 * (gidx - 1))
                    for kp in range(HP):
                        ridx = gidx * HP + kp
                        if kp % 4 == 0:
                            e.wait_ge(S.sems["awc"], AWC0 + gidx * 8 + kp // 4 + 1)
                        slot = ridx % 11
                        last = None
                        for j in range(4):
                            last = e.matmul(pg[:, j * 512:(j + 1) * 512],
                                            SZB[:, slot * 512 + j * 128: slot * 512 + (j + 1) * 128],
                                            ctmp(kp), start=(kp == 0), stop=(kp == HP - 1))
                        if kp % 4 == 3:
                            last.then_inc(s_ait, 1)
                S.emit("t", ft)
                ait_n = S.cur("ait")
                for j in range(4):
                    S.inc("aev", 1)
                def fv(e, tc=tc, grp=grp, gidx=gidx, ait_n=ait_n, BB_all=BB_all,
                       XNW_all=XNW_all, s=S.sem("aev")):
                    pg = PSA if gidx % 2 == 0 else PSB
                    e.wait_ge(S.sems["ait"], ait_n)
                    if gidx == 0:
                        e.wait_ge(S.sems["bb"], BB_all)
                        e.wait_ge(S.sems["xnw"], XNW_all)
                    for j in range(4):
                        nt = grp * 4 + j
                        slot = GC[:, (gidx % 2) * 2048 + j * 512: (gidx % 2) * 2048 + (j + 1) * 512]
                        e.tensor_mul(slot, pg[:, j * 512:(j + 1) * 512], RB[:, tc * TW:(tc + 1) * TW])
                        e.scalar_tensor_tensor(slot, slot, BI[:, nt:nt + 1], ar_x(nt, tc),
                                               ALU.add, ALU.add)
                        e.tensor_add(ar_x(nt, tc), slot, MB[:, tc * TW:(tc + 1) * TW]).then_inc(s, 1)
                S.emit("v", fv)

    def emit_bm_vec():
        S.inc("bmv", 1)
        def f1(e, s=S.sem("bmv")):
            e.wait_ge(S.sems["bmm"], HP)
            e.tensor_single_scalar(MRED[:, 0:128], PSB[:, 0:128], 0.0, ALU.is_gt).then_inc(s, 1)
        S.emit("v", f1)
        S.inc("bmc", 1)
        def f2(e, s=S.sem("bmc")):
            e.wait_ge(S.sems["bmt"], 1)
            e.tensor_copy(BMT[:, :], PSB[:, 512:640]).then_inc(s, 1)
        S.emit("v", f2)
        for cc in range(8):
            S.inc("bhe", 1)
            def f3(e, cc=cc, s=S.sem("bhe")):
                e.wait_ge(S.sems["bhm"], cc + 1)
                if cc == 0:
                    e.wait_ge(S.sems["stm"], HP)
                if cc >= 4:
                    e.wait_ge(S.sems["bhw"], 16 * cc)
                e.tensor_copy(UC[:, (cc % 4) * 512:(cc % 4 + 1) * 512],
                              PSA[:, 1024 + (cc % 2) * 512: 1536 + (cc % 2) * 512]).then_inc(s, 1)
            S.emit("v", f3)
        for cc in range(8):
            S.inc("bhw", 16)
            def f4(e, cc=cc, s=S.sem("bhw")):
                e.wait_ge(S.sems["bhe"], cc + 1)
                e.dma_start(out=bmh_s[:, cc * 512:(cc + 1) * 512],
                            in_=UC[:, (cc % 4) * 512:(cc % 4 + 1) * 512]).then_inc(s, 16)
            S.emit("s", f4)

    for src, dst in ((sgT, sg_b), (zgT, zg_b), (suT, su_b), (zuT, zu_b),
                     (sdT, sd_b), (zdT, zd_b)):
        S.inc("szc", 16)
        S.emit("g", lambda e, a=src, b=dst, s=S.sem("szc"): e.dma_start(out=b, in_=a).then_inc(s, 16))
    for hp in range(HP):
        S.inc("xl", 16)
        S.emit("g", lambda e, hp=hp, s=S.sem("xl"):
               e.dma_start(out=ar_x(hp), in_=xT[hp * 128:(hp + 1) * 128, :]).then_inc(s, 16))

    for src, dst in ((ones_col, ONES[:, :]), (eps_c, EPS[:, :]), (negh_c, NEGH[:, :]),
                     (b4_c, B4[:, :]), (e_c, GC[:, :])):
        S.inc("cst", 16)
        S.emit("y", lambda e, a=src, b=dst, s=S.sem("cst"): e.dma_start(out=b, in_=a).then_inc(s, 16))
    S.inc("cst", 16)
    S.emit("y", lambda e, s=S.sem("cst"): e.dma_start(out=BI[:, :], in_=biasT).then_inc(s, 16))
    N_CST = S.cur("cst")

    def emit_mask_load(j):
        dst = (SLG if j % 2 == 0 else SLU)[:, :].bitcast(I32)
        msem = "mkl%d" % (j % 2)
        S.inc(msem, 16)
        def f(e, j=j, dst=dst, s=S.sem(msem)):
            if j >= 2:
                e.wait_ge(S.sems["mkr"], j - 1)
            e.dma_start(out=dst, in_=mask[j * 128:(j + 1) * 128, :]).then_inc(s, 16)
        S.emit("s", f)

    for j in range(HP):
        emit_square(j, ([("xl", 16 * HP)] if j == 0 else []) + ([("stm", j)] if j >= 1 else []))

    emit_stats_mms([("cst", N_CST)])
    emit_stats_tail(1)
    for j in range(HP):
        emit_mask_load(j)
    for j in range(HP):
        S.inc("bmm", 1)
        def fbm(e, j=j, ste_n=S.cur("ste"), s=S.sem("bmm")):
            if j == 0:
                e.wait_ge(S.sems["ste"], ste_n)
            e.wait_ge(S.sems["mkr"], j + 1)
            e.matmul(PSB[:, 4 * j:4 * j + 4], MRED[:, (j % 2) * 128:(j % 2) * 128 + 128],
                     B4[:, :], start=True, stop=True).then_inc(s, 1)
        S.emit("t", fbm)
    S.inc("cst", 16)
    S.emit("y", lambda e, s=S.sem("cst"): (e.wait_ge(S.sems["bmm"], HP),
                                           e.dma_start(out=MRED[:, 128:256], in_=i128_c).then_inc(s, 16)))
    CST_I128 = S.cur("cst")
    S.inc("bmt", 1)
    S.emit("t", lambda e, s=S.sem("bmt"), ste_n=S.cur("ste"): (e.wait_ge(S.sems["bmv"], 1),
                                           e.wait_ge(S.sems["ste"], ste_n),
                                           e.wait_ge(S.sems["cst"], CST_I128),
                                           e.transpose(PSB[:, 512:640], MRED[:, 0:128],
                                                       MRED[:, 128:256]).then_inc(s, 1)))
    for cc in range(8):
        S.inc("bhm", 1)
        def fbh(e, cc=cc, ste_n=S.cur("ste"), s=S.sem("bhm")):
            if cc == 0:
                e.wait_ge(S.sems["bmc"], 1)
                e.wait_ge(S.sems["ste"], ste_n)
            if cc >= 2:
                e.wait_ge(S.sems["bhe"], cc - 1)
            reg = PSA[:, 1024 + (cc % 2) * 512: 1536 + (cc % 2) * 512]
            e.matmul(reg, BMT[:, :], GC[:, cc * 512:(cc + 1) * 512],
                     start=True, stop=True).then_inc(s, 1)
        S.emit("t", fbh)

    for j in range(HP):
        src3 = (SLG if j % 2 == 0 else SLU)[:, :].bitcast(I32).rearrange("p (g k) -> p g k", k=32)
        S.inc("mkr", 1)
        def fmr(e, j=j, src3=src3, s=S.sem("mkr")):
            e.wait_ge(S.sems["mkl%d" % (j % 2)], 16 * (j // 2 + 1))
            if j >= 2:
                e.wait_ge(S.sems["bmm"], j - 1)
            e.reduce_sum(MRED[:, (j % 2) * 128:(j % 2) * 128 + 128].unsqueeze(2), src3,
                         axis=mybir.AxisListType.X).then_inc(s, 1)
        S.emit("v", fmr)

    emit_bm_vec()

    def fin(*specs):
        for nm, val in specs:
            S.emit("y", lambda e, nm=nm, val=val: e.wait_ge(S.sems[nm], val))

    while True:
        if stage <= 1:
            fin(("ctr", 32), ("bhw", S.cur("bhw")))
            break
        PA1 = (S.cur("mm"), S.cur("ev"), S.cur("hs"), S.cur("hv"),
               (S.cur("ha0"), S.cur("ha1")))
        k_done = 0
        for ci in range(len(CHG)):
            emit_dq_chunk_gu(ci)
            for k in range(k_done, CH_TILE_END[ci]):
                extra_mm = []
                extra_ev = []
                extra_sl = []
                if k == 0:
                    extra_mm = [("ctr", 32), ("ste", S.cur("ste")), ("bmv", 1), ("bmc", 1),
                                ("bhe", 8)]
                    extra_ev = [("bhm", 8), ("bhw", 16 * 8)]
                if k <= 1:
                    extra_sl = [("mkr", HP)]
                emit_phaseA_tile(k, 1, PA1, extra_mm, extra_ev, extra_sl)
            k_done = CH_TILE_END[ci]
        if stage <= 2:
            fin(("ha0", S.cur("ha0")), ("ha1", S.cur("ha1")))
            break

        emit_dq_down()
        emit_wm_build()
        if stage <= 3:
            fin(("ha0", S.cur("ha0")), ("ha1", S.cur("ha1")), ("wmb", S.cur("wmb")))
            break
        emit_xc_spill(1)
        emit_phaseB(1, n_tc=phb_tc, n_grp=phb_grp)
        if stage <= 4:
            fin(("xnw", S.cur("xnw")))
            break

        xnw_all = S.cur("xnw")
        yit_all = S.cur("yit")
        yev_all = S.cur("yev")
        for hp in range(HP):
            S.inc("xrl", 16)
            def fxr(e, hp=hp, xnw_all=xnw_all, yit_all=yit_all, yev_all=yev_all, s=S.sem("xrl")):
                if hp == 0:
                    e.wait_ge(S.sems["xnw"], xnw_all)
                    e.wait_ge(S.sems["yit"], yit_all)
                    e.wait_ge(S.sems["yev"], yev_all)
                e.dma_start(out=ar_x(hp), in_=xnew_s[hp * 128:(hp + 1) * 128, :]).then_inc(s, 16)
            S.emit("g", fxr)

        emit_stats_full(2, [("xrl", S.cur("xrl"))], [("yev", S.cur("yev"))])
        emit_attn()
        if stage <= 5:
            fin(("aev", S.cur("aev")))
            break

        emit_stats_full(3, [("aev", S.cur("aev"))], [("aev", S.cur("aev"))])
        emit_xc_spill(2)

        PA2 = (S.cur("mm"), S.cur("ev"), S.cur("hs"), S.cur("hv"),
               (S.cur("ha0"), S.cur("ha1")))
        for k in range(NI):
            extra_mm = []
            extra_ev = []
            extra_sl = []
            if k == 0:
                extra_mm = [("ctr", S.cur("ctr")), ("aev", S.cur("aev")), ("ste", S.cur("ste"))]
                extra_ev = [("aev", S.cur("aev"))]
                extra_sl = [("ait", S.cur("ait"))]
            emit_phaseA_tile(k, 2, PA2, extra_mm, extra_ev, extra_sl)
        emit_phaseB(2)

        osw_all = S.cur("osw")
        S.emit("y", lambda e, n=osw_all: e.wait_ge(S.sems["osw"], n))
        break

    with nc.Block() as block:
        @block.gpsimd
        def _(eng):
            for f in S.q["g"]:
                f(eng)

        @block.sync
        def _(eng):
            for f in S.q["y"]:
                f(eng)

        @block.scalar
        def _(eng):
            for f in S.q["s"]:
                f(eng)

        @block.vector
        def _(eng):
            for f in S.q["v"]:
                f(eng)

        @block.tensor
        def _(eng):
            for f in S.q["t"]:
                f(eng)

    return nc


_NC_CACHE = [None]


def kernel(L0_attn_data, L0_attn_mask, L0_attn_weight, L0_attn_bias,
           L0_ffn_qw_gate, L0_ffn_qw_up, L0_ffn_qw_down,
           L0_ffn_s_gate, L0_ffn_z_gate, L0_ffn_s_up, L0_ffn_z_up,
           L0_ffn_s_down, L0_ffn_z_down):
    x = np.asarray(L0_attn_data, dtype=np.float32).reshape(NCORE * T, H)
    qgT = np.ascontiguousarray(np.asarray(L0_ffn_qw_gate, dtype=np.int32).T)
    quT = np.ascontiguousarray(np.asarray(L0_ffn_qw_up, dtype=np.int32).T)
    qdT = np.ascontiguousarray(np.asarray(L0_ffn_qw_down, dtype=np.int32).T)
    sgT = np.ascontiguousarray(np.asarray(L0_ffn_s_gate, dtype=np.float32).T)
    zgT = np.ascontiguousarray(np.asarray(L0_ffn_z_gate, dtype=np.float32).T)
    suT = np.ascontiguousarray(np.asarray(L0_ffn_s_up, dtype=np.float32).T)
    zuT = np.ascontiguousarray(np.asarray(L0_ffn_z_up, dtype=np.float32).T)
    sdT = np.ascontiguousarray(np.asarray(L0_ffn_s_down, dtype=np.float32).T)
    zdT = np.ascontiguousarray(np.asarray(L0_ffn_z_down, dtype=np.float32).T)
    maskA = np.ascontiguousarray(np.asarray(L0_attn_mask, dtype=np.int32))
    wT = np.ascontiguousarray(np.asarray(L0_attn_weight, dtype=np.float32).T)
    biasT = np.ascontiguousarray(np.asarray(L0_attn_bias, dtype=np.float32).reshape(32, 128).T)

    if _NC_CACHE[0] is None:
        _NC_CACHE[0] = build_nc()
    nc = _NC_CACHE[0]

    in_maps = []
    for c in range(NCORE):
        xTc = np.ascontiguousarray(x[c * T:(c + 1) * T, :].T)
        in_maps.append({
            "xT": xTc, "qgT": qgT, "quT": quT, "qdT": qdT,
            "sgT": sgT, "zgT": zgT, "suT": suT, "zuT": zuT, "sdT": sdT, "zdT": zdT,
            "mask": maskA, "wT": wT, "biasT": biasT,
        })
    res = run_bass_kernel_spmd(nc, in_maps, core_ids=list(range(NCORE)))
    outs = [np.asarray(res.results[c]["outT"]).T for c in range(NCORE)]
    return np.concatenate(outs, axis=0).reshape(4, 4096, 4096).astype(np.float32)


# version 04
# speedup vs baseline: 1.1405x; 1.1405x over previous
import numpy as np
import ml_dtypes
import concourse.bass as bass
import concourse.mybir as mybir
from concourse.bass_utils import run_bass_kernel_spmd

F32 = mybir.dt.float32
BF16 = mybir.dt.float16
I32 = mybir.dt.int32
AF = mybir.ActivationFunctionType
ALU = mybir.AluOpType
BF = np.float16

H = 4096
HP = 32
I = 11008
NI = 86
T = 2048
NTC = 4
TW = 512
NCORE = 8
CHG = [(0, 512), (512, 1920), (1920, 3328), (3328, 4736), (4736, 6144),
       (6144, 7552), (7552, 8960), (8960, 10368), (10368, 11008)]
CHD = [(0, 1408), (1408, 2816), (2816, 4096)]
CH_TILE_END = [e // 128 for (_, e) in CHG]


class Sch:

    def __init__(self, nc):
        self.nc = nc
        self.q = {"t": [], "v": [], "s": [], "y": [], "g": []}
        self.cnt = {}
        self.sems = {}

    def sem(self, name):
        if name not in self.sems:
            self.sems[name] = self.nc.alloc_semaphore(name)
            self.cnt[name] = 0
        return self.sems[name]

    def cur(self, name):
        self.sem(name)
        return self.cnt[name]

    def inc(self, name, amt):
        self.sem(name)
        self.cnt[name] += amt
        return self.cnt[name]

    def emit(self, qn, fn):
        self.q[qn].append(fn)


def build_nc(stage=99, phb_tc=NTC, phb_grp=8):
    nc = bass.Bass("TRN2", target_bir_lowering=False, detect_race_conditions=False)

    xT = nc.dram_tensor("xT", [H, T], F32, kind="ExternalInput").ap()
    qgT = nc.dram_tensor("qgT", [H, I], I32, kind="ExternalInput").ap()
    quT = nc.dram_tensor("quT", [H, I], I32, kind="ExternalInput").ap()
    qdT = nc.dram_tensor("qdT", [I, H], I32, kind="ExternalInput").ap()
    sgT = nc.dram_tensor("sgT", [32, I], F32, kind="ExternalInput").ap()
    zgT = nc.dram_tensor("zgT", [32, I], F32, kind="ExternalInput").ap()
    suT = nc.dram_tensor("suT", [32, I], F32, kind="ExternalInput").ap()
    zuT = nc.dram_tensor("zuT", [32, I], F32, kind="ExternalInput").ap()
    sdT = nc.dram_tensor("sdT", [NI, H], F32, kind="ExternalInput").ap()
    zdT = nc.dram_tensor("zdT", [NI, H], F32, kind="ExternalInput").ap()
    mask = nc.dram_tensor("mask", [H, H], I32, kind="ExternalInput").ap()
    wT = nc.dram_tensor("wT", [H, H], F32, kind="ExternalInput").ap()
    biasT = nc.dram_tensor("biasT", [128, 32], F32, kind="ExternalInput").ap()
    outT = nc.dram_tensor("outT", [H, T], F32, kind="ExternalOutput").ap()

    wg_s = nc.dram_tensor("wg_s", [H, I], BF16, kind="Internal").ap()
    wu_s = nc.dram_tensor("wu_s", [H, I], BF16, kind="Internal").ap()
    wd_s = nc.dram_tensor("wd_s", [I, H], BF16, kind="Internal").ap()
    sg_b = nc.dram_tensor("sg_b", [32, I], BF16, kind="Internal").ap()
    zg_b = nc.dram_tensor("zg_b", [32, I], BF16, kind="Internal").ap()
    su_b = nc.dram_tensor("su_b", [32, I], BF16, kind="Internal").ap()
    zu_b = nc.dram_tensor("zu_b", [32, I], BF16, kind="Internal").ap()
    sd_b = nc.dram_tensor("sd_b", [NI, H], BF16, kind="Internal").ap()
    zd_b = nc.dram_tensor("zd_b", [NI, H], BF16, kind="Internal").ap()
    hact_s = nc.dram_tensor("hact_s", [I, T], BF16, kind="Internal").ap()
    xcsp_s = nc.dram_tensor("xcsp_s", [H, T], BF16, kind="Internal").ap()
    xnew_s = nc.dram_tensor("xnew_s", [H, T], BF16, kind="Internal").ap()
    wm_s = nc.dram_tensor("wm_s", [H, H], BF16, kind="Internal").ap()
    bmh_s = nc.dram_tensor("bmh_s", [128, H], BF16, kind="Internal").ap()
    mrow_s = nc.dram_tensor("mrow_s", [2, T], F32, kind="Internal").ap()

    ones_col = nc.inline_tensor(np.ones((128, 1), dtype=BF), name="ones_col").ap()
    eps_c = nc.inline_tensor(np.full((128, 1), 1e-5, dtype=np.float32), name="eps_c").ap()
    negh_c = nc.inline_tensor(np.full((128, 1), -0.5, dtype=np.float32), name="negh_c").ap()
    b4_np = np.zeros((128, 4), dtype=np.float32)
    for n in range(128):
        b4_np[n, n // 32] = 1.0
    b4_c = nc.inline_tensor(b4_np, name="b4_c").ap()
    i128_c = nc.inline_tensor(np.eye(128, dtype=np.float32), name="i128_c").ap()
    e_np = np.zeros((128, H), dtype=BF)
    for n in range(H):
        e_np[n // 32, n] = 1.0
    e_c = nc.inline_tensor(e_np, name="e_c").ap()

    AR = nc.alloc_sbuf_tensor("AR", [128, HP * T], BF16)
    SLG = nc.alloc_sbuf_tensor("SLG", [128, 8192], BF16)
    SLU = nc.alloc_sbuf_tensor("SLU", [128, 8192], BF16)
    QS = nc.alloc_sbuf_tensor("QS", [128, 4096], BF16)
    SZB = nc.alloc_sbuf_tensor("SZB", [128, 5632], BF16)
    TMP = nc.alloc_sbuf_tensor("TMP", [128, 2048], BF16)
    GC = nc.alloc_sbuf_tensor("GC", [128, 4096], BF16)
    UC = nc.alloc_sbuf_tensor("UC", [128, 2048], BF16)
    RB = nc.alloc_sbuf_tensor("RB", [128, 2048], BF16)
    MB = nc.alloc_sbuf_tensor("MB", [128, 2048], BF16)
    BI = nc.alloc_sbuf_tensor("BI", [128, 32], F32)
    BMT = nc.alloc_sbuf_tensor("BMT", [128, 128], BF16)
    MRED = nc.alloc_sbuf_tensor("MRED", [128, 256], F32)
    SROW = nc.alloc_sbuf_tensor("SROW", [1, 512], F32)
    ONES = nc.alloc_sbuf_tensor("ONES", [128, 1], BF16)
    EPS = nc.alloc_sbuf_tensor("EPS", [128, 1], F32)
    NEGH = nc.alloc_sbuf_tensor("NEGH", [128, 1], F32)
    B4 = nc.alloc_sbuf_tensor("B4", [128, 4], F32)

    PSA = nc.alloc_psum_tensor("PSA", [128, 2048], F32)
    PSB = nc.alloc_psum_tensor("PSB", [128, 2048], F32)

    S = Sch(nc)

    def ar_x(hp, tc=None):
        if tc is None:
            return AR[:, hp * T:(hp + 1) * T]
        return AR[:, hp * T + tc * TW: hp * T + (tc + 1) * TW]

    def ar_h(it):
        return AR[:, it * TW:(it + 1) * TW]

    def ar_xc(hp):
        base = NI * TW
        return AR[:, base + hp * TW: base + (hp + 1) * TW]

    def pa_gu(tc):
        pg = PSA if tc % 2 == 0 else PSB
        off = 0 if tc < 2 else 1024
        return pg[:, off:off + 512], pg[:, off + 512:off + 1024]

    def emit_square(hp, wait_specs):
        S.inc("stq", 1)
        def f(e, hp=hp, ws=tuple(wait_specs), s=S.sem("stq")):
            for nm, val in ws:
                if val > 0:
                    e.wait_ge(S.sems[nm], val)
            e.activation(UC[:, :], ar_x(hp), AF.Square).then_inc(s, 1)
        S.emit("s", f)

    def emit_stats_mms(extra_wait):
        st0 = S.cur("stq") - HP
        for hp in range(HP):
            S.inc("stm", 1)
            def f(e, hp=hp, st0=st0, ew=tuple(extra_wait), s=S.sem("stm")):
                if hp == 0:
                    for nm, val in ew:
                        if val > 0:
                            e.wait_ge(S.sems[nm], val)
                e.wait_ge(S.sems["stq"], st0 + hp + 1)
                for tc in range(NTC):
                    e.matmul(PSA[0:1, tc * TW:(tc + 1) * TW], ONES[:, :],
                             ar_x(hp, tc), start=(hp == 0), stop=(hp == HP - 1))
                last = None
                for tc in range(NTC):
                    last = e.matmul(PSB[0:1, tc * TW:(tc + 1) * TW], ONES[:, :],
                                    UC[:, tc * TW:(tc + 1) * TW],
                                    start=(hp == 0), stop=(hp == HP - 1))
                last.then_inc(s, 1)
            S.emit("t", f)

    def emit_stats_tail(sub):
        stm_n = S.cur("stm")
        bb_prev = S.cur("bb")
        MRW0 = S.cur("mrw")
        for tc in range(NTC):
            strA = S.inc("str", 1)
            S.inc("mrw", 16)
            mrwA = S.cur("mrw")
            S.inc("srr", 1)
            srrA = S.cur("srr")
            S.inc("str", 1)
            strB = S.cur("str")
            S.inc("mrw", 16)
            last_tc = tc == NTC - 1
            if last_tc:
                S.inc("ste", 1)
            def fv1(e, tc=tc, stm_n=stm_n, strA=strA, mrwA=mrwA, MRW0=MRW0,
                    s_str=S.sem("str"), s_srr=S.sem("srr")):
                if tc == 0:
                    e.wait_ge(S.sems["stm"], stm_n)
                else:
                    e.wait_ge(S.sems["mrw"], MRW0 + 32 * tc)
                e.tensor_single_scalar(SROW[0:1, :], PSA[0:1, tc * TW:(tc + 1) * TW],
                                       1.0 / H, ALU.mult).then_inc(s_str, 1)
                e.wait_ge(S.sems["mrw"], mrwA)
                e.tensor_mul(SROW[0:1, :], SROW[0:1, :], SROW[0:1, :])
                ins = e.scalar_tensor_tensor(SROW[0:1, :], PSB[0:1, tc * TW:(tc + 1) * TW],
                                             1.0 / H, SROW[0:1, :], ALU.mult, ALU.subtract)
                ins.then_inc(s_srr, 1)
            S.emit("v", fv1)
            def fs1(e, tc=tc, strA=strA, srrA=srrA, strB=strB, bb_prev=bb_prev,
                    last_tc=last_tc, s_mrw=S.sem("mrw"), s_str=S.sem("str"),
                    s_ste=S.sem("ste")):
                e.wait_ge(S.sems["str"], strA)
                if bb_prev > 0 and tc == 0:
                    e.wait_ge(S.sems["bb"], bb_prev)
                e.dma_start(out=mrow_s[0:1, tc * TW:(tc + 1) * TW],
                            in_=SROW[0:1, :]).then_inc(s_mrw, 16)
                e.wait_ge(S.sems["srr"], srrA)
                if last_tc:
                    e.sem_inc(s_ste, 1)
                e.activation(SROW[0:1, :], SROW[0:1, :], AF.Ln, bias=EPS[0:1, :])
                e.activation(SROW[0:1, :], SROW[0:1, :], AF.Exp,
                             scale=NEGH[0:1, :]).then_inc(s_str, 1)
                e.wait_ge(S.sems["str"], strB)
                e.dma_start(out=mrow_s[1:2, tc * TW:(tc + 1) * TW],
                            in_=SROW[0:1, :]).then_inc(s_mrw, 16)
            S.emit("s", fs1)
        mrw_n = S.cur("mrw")
        consumer_waits = []
        if sub == 2:
            consumer_waits = [("ev", S.cur("ev")), ("yev", S.cur("yev")), ("ctr", S.cur("ctr"))]
        elif sub == 3:
            consumer_waits = [("aev", S.cur("aev")), ("ctr", S.cur("ctr"))]
        S.inc("bb", 32)
        def fg(e, mrw_n=mrw_n, cw=tuple(consumer_waits), s=S.sem("bb")):
            e.wait_ge(S.sems["mrw"], mrw_n)
            for nm, val in cw:
                if val > 0:
                    e.wait_ge(S.sems[nm], val)
            e.dma_start(out=MB[:, :], in_=mrow_s[0:1, :].broadcast_to([128, T])).then_inc(s, 16)
            e.dma_start(out=RB[:, :], in_=mrow_s[1:2, :].broadcast_to([128, T])).then_inc(s, 16)
        S.emit("g", fg)
        bb_n = S.cur("bb")
        for hp in range(HP):
            S.inc("ctr", 1)
            def fc(e, hp=hp, bb_n=bb_n, stm_n=stm_n, s=S.sem("ctr")):
                if hp == 0:
                    e.wait_ge(S.sems["bb"], bb_n)
                    e.wait_ge(S.sems["stm"], stm_n)
                e.tensor_sub(ar_x(hp), ar_x(hp), MB[:, :]).then_inc(s, 1)
            S.emit("v", fc)

    def emit_stats_full(sub, ready_waits, psum_free_waits):
        stq0 = S.cur("stq")
        stm0 = S.cur("stm")
        for hp in range(HP):
            ws = list(ready_waits) if hp == 0 else []
            if hp >= 1:
                ws.append(("stm", stm0 + hp))
            emit_square(hp, ws)
        emit_stats_mms(psum_free_waits)
        emit_stats_tail(sub)

    dq_state = {"count": 0, "slot_writer": [None, None], "pending_fw": None}

    def emit_dq_iter(qsrc, s_bf, z_bf, wdst, grp, c0, c1, wbase):
        it_par = dq_state["count"] % 2
        dq_state["count"] += 1
        csz = c1 - c0
        qs_v = QS[:, it_par * 2048: it_par * 2048 + csz]
        sb_v = SZB[:, it_par * 2816: it_par * 2816 + csz]
        zb_v = SZB[:, it_par * 2816 + 1408: it_par * 2816 + 1408 + csz]
        n_dqv_gate = S.cur("dqv") - 2 if dq_state["count"] >= 2 else 0
        prev_writer = dq_state["slot_writer"][it_par]
        qsem = "qld%d" % it_par
        ssem = "bls%d" % it_par
        zsem = "blz%d" % it_par
        wsem = wbase + str(it_par)
        S.inc(qsem, 16)
        def fq(e, qsrc=qsrc, grp=grp, c0=c0, c1=c1, qs_v=qs_v, pw=prev_writer,
               s=S.sem(qsem)):
            if pw is not None:
                e.wait_ge(S.sems[pw[0]], pw[1])
            e.dma_start(out=qs_v, in_=qsrc[grp * 128:(grp + 1) * 128, c0:c1]).then_inc(s, 16)
        S.emit("g", fq)
        if dq_state["pending_fw"] is not None:
            S.emit("g", dq_state["pending_fw"])
            dq_state["pending_fw"] = None
        S.inc(ssem, 16)
        def fbs(e, s_bf=s_bf, grp=grp, c0=c0, c1=c1, sb_v=sb_v, nv=n_dqv_gate,
                s=S.sem(ssem)):
            if nv > 0:
                e.wait_ge(S.sems["dqv"], nv)
            e.dma_start(out=sb_v, in_=s_bf[grp:grp + 1, c0:c1].broadcast_to([128, c1 - c0])).then_inc(s, 16)
        S.emit("s", fbs)
        S.inc(zsem, 16)
        def fbz(e, z_bf=z_bf, grp=grp, c0=c0, c1=c1, zb_v=zb_v, nv=n_dqv_gate,
                s=S.sem(zsem)):
            if nv > 0:
                e.wait_ge(S.sems["dqv"], nv)
            e.dma_start(out=zb_v, in_=z_bf[grp:grp + 1, c0:c1].broadcast_to([128, c1 - c0])).then_inc(s, 16)
        S.emit("s", fbz)
        S.inc("dqv", 2)
        nq = S.cur(qsem)
        ns = S.cur(ssem)
        nz = S.cur(zsem)
        def fv(e, qs_v=qs_v, sb_v=sb_v, zb_v=zb_v, nq=nq, ns=ns, nz=nz, qsem=qsem,
               ssem=ssem, zsem=zsem, s=S.sem("dqv")):
            e.wait_ge(S.sems[qsem], nq)
            e.wait_ge(S.sems[ssem], ns)
            e.wait_ge(S.sems[zsem], nz)
            e.tensor_sub(qs_v, qs_v, zb_v).then_inc(s, 1)
            e.tensor_mul(qs_v, qs_v, sb_v).then_inc(s, 1)
        S.emit("v", fv)
        nv2 = S.cur("dqv")
        S.inc(wsem, 16)
        dq_state["slot_writer"][it_par] = (wsem, S.cur(wsem))
        def fw(e, wdst=wdst, grp=grp, c0=c0, c1=c1, qs_v=qs_v, nv=nv2, s=S.sem(wsem)):
            e.wait_ge(S.sems["dqv"], nv)
            e.dma_start(out=wdst[grp * 128:(grp + 1) * 128, c0:c1], in_=qs_v).then_inc(s, 16)
        dq_state["pending_fw"] = fw

    def dq_worklist():
        work = []
        for (c0, c1) in CHG:
            for qsrc, s_bf, z_bf, wdst in ((qgT, sg_b, zg_b, wg_s), (quT, su_b, zu_b, wu_s)):
                for grp in range(32):
                    work.append(("gu", (qsrc, s_bf, z_bf, wdst, grp, c0, c1, "dqw")))
            work.append(("ms", None))
        for gi in range(NI):
            for (c0, c1) in CHD:
                work.append(("dn", (qdT, sd_b, zd_b, wd_s, gi, c0, c1, "dqd")))
        for kp in range(HP):
            work.append(("wm", kp))
        return work

    def emit_dq_chunk_milestone():
        if dq_state["pending_fw"] is not None:
            S.emit("g", dq_state["pending_fw"])
            dq_state["pending_fw"] = None
        w0 = S.cur("dqw0")
        w1 = S.cur("dqw1")
        S.inc("dqc", 1)
        def fms(e, w0=w0, w1=w1, s=S.sem("dqc")):
            if w0 > 0:
                e.wait_ge(S.sems["dqw0"], w0)
            if w1 > 0:
                e.wait_ge(S.sems["dqw1"], w1)
            e.sem_inc(s, 1)
        S.emit("g", fms)

    def emit_phaseA_tile(k, p, bases, extra_mm_waits=(), extra_ev_waits=(),
                         extra_sl_waits=()):
        MM0, EV0, HS0, HV0, HA0 = bases
        buf = k % 2
        wg3 = wg_s.rearrange("(hp p) i -> p hp i", p=128)
        wu3 = wu_s.rearrange("(hp p) i -> p hp i", p=128)
        if p == 1:
            ci = 0
            while CH_TILE_END[ci] <= k:
                ci += 1
            dqw_w = [("dqc", ci + 1)]
        else:
            dqw_w = []
        mm_n = MM0 + 4 * (k - 1) if k >= 2 else 0
        slsem = "sl%d" % buf
        for src3, dstT in ((wg3, SLG), (wu3, SLU)):
            S.inc(slsem, 16)
            def fl(e, k=k, buf=buf, src3=src3, dstT=dstT, dqw_w=tuple(dqw_w), mm_n=mm_n,
                   esw=tuple(extra_sl_waits), s=S.sem(slsem)):
                for nm, val in esw:
                    if val > 0:
                        e.wait_ge(S.sems[nm], val)
                for nm, val in dqw_w:
                    if val > 0:
                        e.wait_ge(S.sems[nm], val)
                if mm_n:
                    e.wait_ge(S.sems["mm"], mm_n)
                dst = dstT[:, buf * 4096:(buf + 1) * 4096].rearrange("p (hp i) -> p hp i", hp=HP)
                e.dma_start(out=dst, in_=src3[:, :, k * 128:(k + 1) * 128]).then_inc(s, 16)
            S.emit("y", fl)
        sl_n = S.cur(slsem)
        for tc in range(NTC):
            S.inc("mm", 1)
            def ft(e, k=k, tc=tc, buf=buf, sl_n=sl_n, EV0=EV0, emw=tuple(extra_mm_waits),
                   s=S.sem("mm")):
                if tc == 0:
                    e.wait_ge(S.sems["sl%d" % buf], sl_n)
                    if k == 0:
                        for nm, val in emw:
                            if val > 0:
                                e.wait_ge(S.sems[nm], val)
                if k > 0:
                    e.wait_ge(S.sems["ev"], EV0 + 8 * (k - 1) + 2 * tc + 2)
                gs, us = pa_gu(tc)
                for hp in range(HP):
                    lg = SLG[:, buf * 4096 + hp * 128: buf * 4096 + (hp + 1) * 128]
                    lu = SLU[:, buf * 4096 + hp * 128: buf * 4096 + (hp + 1) * 128]
                    rx = ar_x(hp, tc)
                    e.matmul(gs, lg, rx, start=(hp == 0), stop=(hp == HP - 1))
                    mmi = e.matmul(us, lu, rx, start=(hp == 0), stop=(hp == HP - 1))
                    if hp == HP - 1:
                        mmi.then_inc(s, 1)
            S.emit("t", ft)
        hs_list = []
        for tc in range(NTC):
            S.inc("ev", 2)
        for tc in range(NTC):
            S.inc("hs", 1)
            hs_list.append(S.cur("hs"))
        for tc in range(NTC):
            S.inc("hv", 1)
        def fv(e, k=k, p=p, buf=buf, MM0=MM0, HA0=HA0, hs_list=tuple(hs_list),
               eew=tuple(extra_ev_waits), s_ev=S.sem("ev"), s_hv=S.sem("hv")):
            for tc in range(NTC):
                e.wait_ge(S.sems["mm"], MM0 + 4 * k + tc + 1)
                if tc == 0:
                    if k >= 2:
                        e.wait_ge(S.sems["ha%d" % buf], HA0[buf] + 16 * ((k - 2) // 2 + 1))
                    if k == 0:
                        for nm, val in eew:
                            if val > 0:
                                e.wait_ge(S.sems[nm], val)
                gs, us = pa_gu(tc)
                gslot = GC[:, buf * 2048 + tc * TW: buf * 2048 + (tc + 1) * TW]
                e.tensor_mul(gslot, gs, RB[:, tc * TW:(tc + 1) * TW]).then_inc(s_ev, 1)
                e.tensor_mul(UC[:, tc * TW:(tc + 1) * TW], us,
                             RB[:, tc * TW:(tc + 1) * TW]).then_inc(s_ev, 1)
            for tc in range(NTC):
                e.wait_ge(S.sems["hs"], hs_list[tc])
                gslot = GC[:, buf * 2048 + tc * TW: buf * 2048 + (tc + 1) * TW]
                e.tensor_mul(gslot, gslot, UC[:, tc * TW:(tc + 1) * TW]).then_inc(s_hv, 1)
        S.emit("v", fv)
        S.inc("ha%d" % buf, 16)
        def fsc(e, k=k, buf=buf, EV0=EV0, HV0=HV0, s_hs=S.sem("hs"), s_ha=S.sem("ha%d" % buf)):
            for tc in range(NTC):
                e.wait_ge(S.sems["ev"], EV0 + 8 * k + 2 * tc + 1)
                gslot = GC[:, buf * 2048 + tc * TW: buf * 2048 + (tc + 1) * TW]
                e.activation(gslot, gslot, AF.Silu).then_inc(s_hs, 1)
            e.wait_ge(S.sems["hv"], HV0 + 4 * k + 4)
            e.dma_start(out=hact_s[k * 128:(k + 1) * 128, :],
                        in_=GC[:, buf * 2048:(buf + 1) * 2048]).then_inc(s_ha, 16)
        S.emit("s", fsc)

    def emit_xc_spill(p):
        ctr_n = S.cur("ctr")
        xcr_prev = S.cur("xcr")
        for hp in range(HP):
            S.inc("xcw", 16)
            def f(e, hp=hp, ctr_n=ctr_n, xcr_prev=xcr_prev, s=S.sem("xcw")):
                if hp == 0:
                    e.wait_ge(S.sems["ctr"], ctr_n)
                    if xcr_prev > 0:
                        e.wait_ge(S.sems["xcr"], xcr_prev)
                e.dma_start(out=xcsp_s[hp * 128:(hp + 1) * 128, :], in_=ar_x(hp)).then_inc(s, 16)
            S.emit("s", f)

    def emit_phaseB(p, n_tc=NTC, n_grp=8):
        MM_all = S.cur("mm")
        XCW_all = S.cur("xcw")
        HA_all = (S.cur("ha0"), S.cur("ha1"))
        DQD_all = (S.cur("dqd0"), S.cur("dqd1"))
        HCL0 = S.cur("hcl")
        XCR0 = S.cur("xcr")

        YIT0 = S.cur("yit")

        YEV0 = S.cur("yev")
        wsem = "xnw" if p == 1 else "osw"
        XNW0 = S.cur(wsem)
        wdsem = "wdl%d" % p
        WDC0 = S.cur("wdc%d" % p)
        EV_all = S.cur("ev")
        AEV_all = S.cur("aev")
        STE_all = S.cur("ste")

        def out_slot(par, j, p):
            if p == 1:
                return GC[:, par * 2048 + j * 512: par * 2048 + (j + 1) * 512]
            if par == 0:
                return GC[:, :].bitcast(F32)[:, j * 512:(j + 1) * 512]
            if j < 2:
                return UC[:, :].bitcast(F32)[:, j * 512:(j + 1) * 512]
            return TMP[:, :].bitcast(F32)[:, (j - 2) * 512:(j - 1) * 512]

        for tc in range(n_tc):
            yit_w = YIT0 + 88 * tc
            yev_w = YEV0 + 32 * tc
            for it in range(NI):
                S.inc("hcl", 16)
                def fh(e, it=it, tc=tc, yit_w=yit_w, yev_w=yev_w, MM_all=MM_all,
                       XCW_all=XCW_all, HA_all=HA_all, s=S.sem("hcl")):
                    if it == 0:
                        if tc == 0:
                            e.wait_ge(S.sems["mm"], MM_all)
                            e.wait_ge(S.sems["xcw"], XCW_all)
                            e.wait_ge(S.sems["ha0"], HA_all[0])
                            e.wait_ge(S.sems["ha1"], HA_all[1])
                        else:
                            e.wait_ge(S.sems["yit"], yit_w)
                            e.wait_ge(S.sems["yev"], yev_w)
                    e.dma_start(out=ar_h(it),
                                in_=hact_s[it * 128:(it + 1) * 128, tc * TW:(tc + 1) * TW]).then_inc(s, 16)
                S.emit("g", fh)
            for hp in range(HP):
                S.inc("xcr", 16)
                def fx(e, hp=hp, tc=tc, s=S.sem("xcr")):
                    e.dma_start(out=ar_xc(hp),
                                in_=xcsp_s[hp * 128:(hp + 1) * 128, tc * TW:(tc + 1) * TW]).then_inc(s, 16)
                S.emit("g", fx)
            hcl_n = S.cur("hcl")
            xcr_n = S.cur("xcr")

            for grp in range(n_grp):
                gidx = tc * 8 + grp
                for it in range(NI):
                    ridx = gidx * NI + it
                    slot = ridx % 16
                    S.inc(wdsem, 16)
                    def fw(e, it=it, grp=grp, slot=slot, ridx=ridx, p=p,
                           YIT0=YIT0, DQD_all=DQD_all, MM_all=MM_all, wdsem=wdsem,
                           s=S.sem(wdsem)):
                        if ridx == 0:
                            if p == 1:
                                e.wait_ge(S.sems["dqd0"], DQD_all[0])
                                e.wait_ge(S.sems["dqd1"], DQD_all[1])
                            e.wait_ge(S.sems["mm"], MM_all)
                        if ridx >= 16:
                            g2, it2 = divmod(ridx - 16, NI)
                            e.wait_ge(S.sems["yit"], YIT0 + g2 * 11 + it2 // 8 + 1)
                        e.dma_start(out=SLG[:, slot * 512:(slot + 1) * 512],
                                    in_=wd_s[it * 128:(it + 1) * 128,
                                             grp * 512:(grp + 1) * 512]).then_inc(s, 16)
                    S.emit("y", fw)
                    if it % 8 == 7 or it == NI - 1:
                        cl_end = gidx * NI + min((it // 8 + 1) * 8, NI)
                        S.inc("wdc%d" % p, 1)
                        def fm(e, n16=16 * cl_end, wdsem=wdsem,
                               s=S.sem("wdc%d" % p)):
                            e.wait_ge(S.sems[wdsem], n16)
                            e.sem_inc(s, 1)
                        S.emit("y", fm)
                for it in range(NI):
                    if it % 8 == 7 or it == NI - 1:
                        S.inc("yit", 1)
                def ft(e, tc=tc, grp=grp, gidx=gidx, p=p, hcl_n=hcl_n,
                       wdsem=wdsem, WDC0=WDC0, YEV0=YEV0, EV_all=EV_all, AEV_all=AEV_all,
                       STE_all=STE_all, s_yit=S.sem("yit")):
                    pg = PSA if gidx % 2 == 0 else PSB
                    if grp == 0:
                        e.wait_ge(S.sems["hcl"], hcl_n)
                    if gidx < 2:
                        if p == 1:
                            e.wait_ge(S.sems["ev"], EV_all)
                        else:
                            e.wait_ge(S.sems["aev"], AEV_all)
                        e.wait_ge(S.sems["ste"], STE_all)
                    else:
                        e.wait_ge(S.sems["yev"], YEV0 + 4 * (gidx - 1))
                    for it in range(NI):
                        ridx = gidx * NI + it
                        if it % 8 == 0:
                            e.wait_ge(S.sems["wdc%d" % p], WDC0 + gidx * 11 + it // 8 + 1)
                        slot = ridx % 16
                        last = None
                        for j in range(4):
                            last = e.matmul(pg[:, j * 512:(j + 1) * 512],
                                            SLG[:, slot * 512 + j * 128: slot * 512 + (j + 1) * 128],
                                            ar_h(it), start=(it == 0), stop=(it == NI - 1))
                        if it % 8 == 7 or it == NI - 1:
                            last.then_inc(s_yit, 1)
                S.emit("t", ft)
                yit_n = S.cur("yit")
                for j in range(4):
                    S.inc("yev", 1)
                yev_after = S.cur("yev")
                def fv(e, tc=tc, grp=grp, gidx=gidx, p=p, yit_n=yit_n, xcr_n=xcr_n,
                       HA_all=HA_all, XNW0=XNW0, wsem=wsem, s=S.sem("yev")):
                    pg = PSA if gidx % 2 == 0 else PSB
                    e.wait_ge(S.sems["yit"], yit_n)
                    if gidx == 0:
                        e.wait_ge(S.sems["xcr"], xcr_n)
                        e.wait_ge(S.sems["ha0"], HA_all[0])
                        e.wait_ge(S.sems["ha1"], HA_all[1])
                    elif grp == 0:
                        e.wait_ge(S.sems["xcr"], xcr_n)
                    if gidx >= 2:
                        e.wait_ge(S.sems[wsem], XNW0 + 16 * 4 * (gidx - 1))
                    for j in range(4):
                        hp = grp * 4 + j
                        slot = out_slot(gidx % 2, j, p)
                        e.scalar_tensor_tensor(slot, pg[:, j * 512:(j + 1) * 512], 0.5,
                                               ar_xc(hp), ALU.mult, ALU.add)
                        e.tensor_add(slot, slot, MB[:, tc * TW:(tc + 1) * TW]).then_inc(s, 1)
                S.emit("v", fv)
                for j in range(4):
                    hp = grp * 4 + j
                    S.inc(wsem, 16)
                    def fst(e, tc=tc, gidx=gidx, j=j, hp=hp, p=p, yev_after=yev_after,
                            s=S.sem(wsem)):
                        e.wait_ge(S.sems["yev"], yev_after - 3 + j)
                        slot = out_slot(gidx % 2, j, p)
                        if p == 1:
                            dst = xnew_s[hp * 128:(hp + 1) * 128, tc * TW:(tc + 1) * TW]
                        else:
                            dst = outT[hp * 128:(hp + 1) * 128, tc * TW:(tc + 1) * TW]
                        e.dma_start(out=dst, in_=slot).then_inc(s, 16)
                    S.emit("s", fst)

    def emit_wm_piece(kp):
        dq_all = [(nm, S.cur(nm)) for nm in ("dqw0", "dqw1", "dqd0", "dqd1")]
        bhw_all = S.cur("bhw")
        dqv_all = S.cur("dqv")
        if True:
            S.inc("wtl", 16)
            def fg(e, kp=kp, dq_all=tuple(dq_all), dqv_all=dqv_all, s=S.sem("wtl")):
                if kp == 0:
                    for nm, val in dq_all:
                        if val > 0:
                            e.wait_ge(S.sems[nm], val)
                    e.wait_ge(S.sems["dqv"], dqv_all)
                else:
                    e.wait_ge(S.sems["wmv"], 2 * kp)
                e.dma_start(out=QS[:, :], in_=wT[kp * 128:(kp + 1) * 128, :]).then_inc(s, 16)
            S.emit("g", fg)
            S.inc("wte", 16 * 4)
            def fge(e, kp=kp, bhw_all=bhw_all, dqv_all=dqv_all, s=S.sem("wte")):
                if kp == 0:
                    e.wait_ge(S.sems["bhw"], bhw_all)
                    e.wait_ge(S.sems["dqv"], dqv_all)
                else:
                    e.wait_ge(S.sems["wmv"], 2 * kp)
                for j in range(4):
                    kb = kp * 4 + j
                    e.dma_start(out=SZB[32 * j:32 * (j + 1), 0:4096],
                                in_=bmh_s[kb:kb + 1, :].broadcast_to([32, 4096])).then_inc(s, 16)
            S.emit("s", fge)
            wtl_n = S.cur("wtl")
            wte_n = S.cur("wte")
            for h in range(2):
                S.inc("wmv", 1)
                wmb_need = 16 * (2 * kp + h)
                def fv(e, kp=kp, h=h, wtl_n=wtl_n, wte_n=wte_n, wmb_need=wmb_need,
                       s=S.sem("wmv")):
                    if h == 0:
                        e.wait_ge(S.sems["wtl"], wtl_n)
                        e.wait_ge(S.sems["wte"], wte_n)
                    if wmb_need > 0:
                        e.wait_ge(S.sems["wmb"], wmb_need)
                    e.tensor_mul(TMP[:, :], QS[:, h * 2048:(h + 1) * 2048],
                                 SZB[:, h * 2048:(h + 1) * 2048]).then_inc(s, 1)
                S.emit("v", fv)
                wmv_n = S.cur("wmv")
                S.inc("wmb", 16)
                def fs2(e, kp=kp, h=h, wmv_n=wmv_n, s=S.sem("wmb")):
                    e.wait_ge(S.sems["wmv"], wmv_n)
                    e.dma_start(out=wm_s[kp * 128:(kp + 1) * 128, h * 2048:(h + 1) * 2048],
                                in_=TMP[:, :]).then_inc(s, 16)
                S.emit("s", fs2)

    def emit_attn():
        XRL_all = S.cur("xrl")
        WMB_all = S.cur("wmb")
        YIT_all = S.cur("yit")
        YEV_all = S.cur("yev")
        BB_all = S.cur("bb")
        CTR_all = S.cur("ctr")
        AWL0 = S.cur("awl")
        AWC0 = S.cur("awc")
        AIT0 = S.cur("ait")
        AEV0 = S.cur("aev")
        XNW_all = S.cur("xnw")
        STE_all = S.cur("ste")

        def ctmp(kp):
            tens = SLG if kp < 16 else SLU
            return tens[:, (kp % 16) * 512:(kp % 16 + 1) * 512]

        for tc in range(NTC):
            ait_w = AIT0 + 64 * tc
            for kp in range(HP):
                S.inc("atc", 1)
            atc_n = S.cur("atc")
            def fcp(e, tc=tc, ait_w=ait_w, XRL_all=XRL_all, YIT_all=YIT_all,
                    CTR_all=CTR_all, YEV_all=YEV_all, s=S.sem("atc")):
                if tc == 0:
                    e.wait_ge(S.sems["xrl"], XRL_all)
                    e.wait_ge(S.sems["yit"], YIT_all)
                    e.wait_ge(S.sems["yev"], YEV_all)
                    e.wait_ge(S.sems["ctr"], CTR_all)
                else:
                    e.wait_ge(S.sems["ait"], ait_w)
                for kp in range(HP):
                    e.tensor_copy(ctmp(kp), ar_x(kp, tc)).then_inc(s, 1)
            S.emit("v", fcp)

            for grp in range(8):
                gidx = tc * 8 + grp
                for kp in range(HP):
                    ridx = gidx * HP + kp
                    slot = ridx % 11
                    S.inc("awl", 16)
                    def fw(e, kp=kp, grp=grp, slot=slot, ridx=ridx, WMB_all=WMB_all,
                           AIT0=AIT0, s=S.sem("awl")):
                        if ridx == 0:
                            e.wait_ge(S.sems["wmb"], WMB_all)
                            e.wait_ge(S.sems["wmv"], 2 * HP)
                        if ridx >= 11:
                            g2, kp2 = divmod(ridx - 11, HP)
                            e.wait_ge(S.sems["ait"], AIT0 + g2 * 8 + kp2 // 4 + 1)
                        e.dma_start(out=SZB[:, slot * 512:(slot + 1) * 512],
                                    in_=wm_s[kp * 128:(kp + 1) * 128,
                                             grp * 512:(grp + 1) * 512]).then_inc(s, 16)
                    S.emit("y", fw)
                    if kp % 4 == 3:
                        cl_end = gidx * HP + (kp // 4 + 1) * 4
                        S.inc("awc", 1)
                        def fm(e, n16=16 * cl_end, s=S.sem("awc")):
                            e.wait_ge(S.sems["awl"], n16)
                            e.sem_inc(s, 1)
                        S.emit("y", fm)
                for kp in range(HP):
                    if kp % 4 == 3:
                        S.inc("ait", 1)
                def ft(e, tc=tc, grp=grp, gidx=gidx, atc_n=atc_n, AWC0=AWC0,
                       AEV0=AEV0, YEV_all=YEV_all, STE_all=STE_all,
                       s_ait=S.sem("ait")):
                    pg = PSA if gidx % 2 == 0 else PSB
                    if grp == 0:
                        e.wait_ge(S.sems["atc"], atc_n)
                    if gidx < 2:
                        e.wait_ge(S.sems["yev"], YEV_all)
                        e.wait_ge(S.sems["ste"], STE_all)
                    else:
                        e.wait_ge(S.sems["aev"], AEV0 + 4 * (gidx - 1))
                    for kp in range(HP):
                        ridx = gidx * HP + kp
                        if kp % 4 == 0:
                            e.wait_ge(S.sems["awc"], AWC0 + gidx * 8 + kp // 4 + 1)
                        slot = ridx % 11
                        last = None
                        for j in range(4):
                            last = e.matmul(pg[:, j * 512:(j + 1) * 512],
                                            SZB[:, slot * 512 + j * 128: slot * 512 + (j + 1) * 128],
                                            ctmp(kp), start=(kp == 0), stop=(kp == HP - 1))
                        if kp % 4 == 3:
                            last.then_inc(s_ait, 1)
                S.emit("t", ft)
                ait_n = S.cur("ait")
                for j in range(4):
                    S.inc("aev", 1)
                def fv(e, tc=tc, grp=grp, gidx=gidx, ait_n=ait_n, BB_all=BB_all,
                       XNW_all=XNW_all, s=S.sem("aev")):
                    pg = PSA if gidx % 2 == 0 else PSB
                    e.wait_ge(S.sems["ait"], ait_n)
                    if gidx == 0:
                        e.wait_ge(S.sems["bb"], BB_all)
                        e.wait_ge(S.sems["xnw"], XNW_all)
                    for j in range(4):
                        nt = grp * 4 + j
                        slot = GC[:, (gidx % 2) * 2048 + j * 512: (gidx % 2) * 2048 + (j + 1) * 512]
                        e.tensor_mul(slot, pg[:, j * 512:(j + 1) * 512], RB[:, tc * TW:(tc + 1) * TW])
                        e.scalar_tensor_tensor(slot, slot, BI[:, nt:nt + 1], ar_x(nt, tc),
                                               ALU.add, ALU.add)
                        e.tensor_add(ar_x(nt, tc), slot, MB[:, tc * TW:(tc + 1) * TW]).then_inc(s, 1)
                S.emit("v", fv)

    def emit_bm_vec():
        S.inc("bmv", 1)
        def f1(e, s=S.sem("bmv")):
            e.wait_ge(S.sems["bmm"], HP)
            e.tensor_single_scalar(MRED[:, 0:128], PSB[:, 0:128], 0.0, ALU.is_gt).then_inc(s, 1)
        S.emit("v", f1)
        S.inc("bmc", 1)
        def f2(e, s=S.sem("bmc")):
            e.wait_ge(S.sems["bmt"], 1)
            e.tensor_copy(BMT[:, :], PSB[:, 512:640]).then_inc(s, 1)
        S.emit("v", f2)
        for cc in range(8):
            S.inc("bhe", 1)
            def f3(e, cc=cc, s=S.sem("bhe")):
                e.wait_ge(S.sems["bhm"], cc + 1)
                if cc == 0:
                    e.wait_ge(S.sems["stm"], HP)
                if cc >= 4:
                    e.wait_ge(S.sems["bhw"], 16 * cc)
                e.tensor_copy(UC[:, (cc % 4) * 512:(cc % 4 + 1) * 512],
                              PSA[:, 1024 + (cc % 2) * 512: 1536 + (cc % 2) * 512]).then_inc(s, 1)
            S.emit("v", f3)
        for cc in range(8):
            S.inc("bhw", 16)
            def f4(e, cc=cc, s=S.sem("bhw")):
                e.wait_ge(S.sems["bhe"], cc + 1)
                e.dma_start(out=bmh_s[:, cc * 512:(cc + 1) * 512],
                            in_=UC[:, (cc % 4) * 512:(cc % 4 + 1) * 512]).then_inc(s, 16)
            S.emit("s", f4)

    for src, dst in ((sgT, sg_b), (zgT, zg_b), (suT, su_b), (zuT, zu_b),
                     (sdT, sd_b), (zdT, zd_b)):
        S.inc("szc", 16)
        S.emit("g", lambda e, a=src, b=dst, s=S.sem("szc"): e.dma_start(out=b, in_=a).then_inc(s, 16))
    for hp in range(HP):
        S.inc("xl", 16)
        S.emit("g", lambda e, hp=hp, s=S.sem("xl"):
               e.dma_start(out=ar_x(hp), in_=xT[hp * 128:(hp + 1) * 128, :]).then_inc(s, 16))

    for src, dst in ((ones_col, ONES[:, :]), (eps_c, EPS[:, :]), (negh_c, NEGH[:, :]),
                     (b4_c, B4[:, :]), (e_c, GC[:, :])):
        S.inc("cst", 16)
        S.emit("y", lambda e, a=src, b=dst, s=S.sem("cst"): e.dma_start(out=b, in_=a).then_inc(s, 16))
    S.inc("cst", 16)
    S.emit("y", lambda e, s=S.sem("cst"): e.dma_start(out=BI[:, :], in_=biasT).then_inc(s, 16))
    N_CST = S.cur("cst")

    def emit_mask_load(j):
        dst = (SLG if j % 2 == 0 else SLU)[:, :].bitcast(I32)
        msem = "mkl%d" % (j % 2)
        S.inc(msem, 16)
        def f(e, j=j, dst=dst, s=S.sem(msem)):
            if j >= 2:
                e.wait_ge(S.sems["mkr"], j - 1)
            e.dma_start(out=dst, in_=mask[j * 128:(j + 1) * 128, :]).then_inc(s, 16)
        S.emit("s", f)

    for j in range(HP):
        emit_square(j, ([("xl", 16 * HP)] if j == 0 else []) + ([("stm", j)] if j >= 1 else []))

    emit_stats_mms([("cst", N_CST)])
    emit_stats_tail(1)
    for j in range(HP):
        emit_mask_load(j)
    for j in range(HP):
        S.inc("bmm", 1)
        def fbm(e, j=j, ste_n=S.cur("ste"), s=S.sem("bmm")):
            if j == 0:
                e.wait_ge(S.sems["ste"], ste_n)
            e.wait_ge(S.sems["mkr"], j + 1)
            e.matmul(PSB[:, 4 * j:4 * j + 4], MRED[:, (j % 2) * 128:(j % 2) * 128 + 128],
                     B4[:, :], start=True, stop=True).then_inc(s, 1)
        S.emit("t", fbm)
    S.inc("cst", 16)
    S.emit("y", lambda e, s=S.sem("cst"): (e.wait_ge(S.sems["bmm"], HP),
                                           e.dma_start(out=MRED[:, 128:256], in_=i128_c).then_inc(s, 16)))
    CST_I128 = S.cur("cst")
    S.inc("bmt", 1)
    S.emit("t", lambda e, s=S.sem("bmt"), ste_n=S.cur("ste"): (e.wait_ge(S.sems["bmv"], 1),
                                           e.wait_ge(S.sems["ste"], ste_n),
                                           e.wait_ge(S.sems["cst"], CST_I128),
                                           e.transpose(PSB[:, 512:640], MRED[:, 0:128],
                                                       MRED[:, 128:256]).then_inc(s, 1)))
    for cc in range(8):
        S.inc("bhm", 1)
        def fbh(e, cc=cc, ste_n=S.cur("ste"), s=S.sem("bhm")):
            if cc == 0:
                e.wait_ge(S.sems["bmc"], 1)
                e.wait_ge(S.sems["ste"], ste_n)
            if cc >= 2:
                e.wait_ge(S.sems["bhe"], cc - 1)
            reg = PSA[:, 1024 + (cc % 2) * 512: 1536 + (cc % 2) * 512]
            e.matmul(reg, BMT[:, :], GC[:, cc * 512:(cc + 1) * 512],
                     start=True, stop=True).then_inc(s, 1)
        S.emit("t", fbh)

    for j in range(HP):
        src3 = (SLG if j % 2 == 0 else SLU)[:, :].bitcast(I32).rearrange("p (g k) -> p g k", k=32)
        S.inc("mkr", 1)
        def fmr(e, j=j, src3=src3, s=S.sem("mkr")):
            e.wait_ge(S.sems["mkl%d" % (j % 2)], 16 * (j // 2 + 1))
            if j >= 2:
                e.wait_ge(S.sems["bmm"], j - 1)
            e.reduce_sum(MRED[:, (j % 2) * 128:(j % 2) * 128 + 128].unsqueeze(2), src3,
                         axis=mybir.AxisListType.X).then_inc(s, 1)
        S.emit("v", fmr)

    emit_bm_vec()

    def fin(*specs):
        for nm, val in specs:
            S.emit("y", lambda e, nm=nm, val=val: e.wait_ge(S.sems[nm], val))

    while True:
        if stage <= 1:
            fin(("ctr", 32), ("bhw", S.cur("bhw")))
            break
        PA1 = (S.cur("mm"), S.cur("ev"), S.cur("hs"), S.cur("hv"),
               (S.cur("ha0"), S.cur("ha1")))
        work = dq_worklist()
        pos = [0]

        def pump_one():
            kind, arg = work[pos[0]]
            pos[0] += 1
            if kind in ("gu", "dn"):
                emit_dq_iter(*arg)
            elif kind == "ms":
                emit_dq_chunk_milestone()
            else:
                if dq_state["pending_fw"] is not None:
                    S.emit("g", dq_state["pending_fw"])
                    dq_state["pending_fw"] = None
                emit_wm_piece(arg)

        def gu_iters_needed(tile):
            ci = 0
            while CH_TILE_END[ci] <= tile:
                ci += 1
            return 65 * (ci + 1)

        for k in range(NI):
            while pos[0] < gu_iters_needed(min(k + 11, NI - 1)):
                pump_one()
            for _ in range(7):
                if pos[0] < len(work):
                    pump_one()
            extra_mm = []
            extra_ev = []
            extra_sl = []
            if k == 0:
                extra_mm = [("ctr", 32), ("ste", S.cur("ste")), ("bmv", 1), ("bmc", 1),
                            ("bhe", 8)]
                extra_ev = [("bhm", 8), ("bhw", 16 * 8)]
            if k <= 1:
                extra_sl = [("mkr", HP)]
            emit_phaseA_tile(k, 1, PA1, extra_mm, extra_ev, extra_sl)
        while pos[0] < len(work):
            pump_one()
        if dq_state["pending_fw"] is not None:
            S.emit("g", dq_state["pending_fw"])
            dq_state["pending_fw"] = None
        if stage <= 3:
            fin(("ha0", S.cur("ha0")), ("ha1", S.cur("ha1")), ("wmb", S.cur("wmb")))
            break
        emit_xc_spill(1)
        emit_phaseB(1, n_tc=phb_tc, n_grp=phb_grp)
        if stage <= 4:
            fin(("xnw", S.cur("xnw")))
            break

        xnw_all = S.cur("xnw")
        yit_all = S.cur("yit")
        yev_all = S.cur("yev")
        for hp in range(HP):
            S.inc("xrl", 16)
            def fxr(e, hp=hp, xnw_all=xnw_all, yit_all=yit_all, yev_all=yev_all, s=S.sem("xrl")):
                if hp == 0:
                    e.wait_ge(S.sems["xnw"], xnw_all)
                    e.wait_ge(S.sems["yit"], yit_all)
                    e.wait_ge(S.sems["yev"], yev_all)
                e.dma_start(out=ar_x(hp), in_=xnew_s[hp * 128:(hp + 1) * 128, :]).then_inc(s, 16)
            S.emit("g", fxr)

        emit_stats_full(2, [("xrl", S.cur("xrl"))], [("yev", S.cur("yev"))])
        emit_attn()
        if stage <= 5:
            fin(("aev", S.cur("aev")))
            break

        emit_stats_full(3, [("aev", S.cur("aev"))], [("aev", S.cur("aev"))])
        emit_xc_spill(2)

        PA2 = (S.cur("mm"), S.cur("ev"), S.cur("hs"), S.cur("hv"),
               (S.cur("ha0"), S.cur("ha1")))
        for k in range(NI):
            extra_mm = []
            extra_ev = []
            extra_sl = []
            if k == 0:
                extra_mm = [("ctr", S.cur("ctr")), ("aev", S.cur("aev")), ("ste", S.cur("ste"))]
                extra_ev = [("aev", S.cur("aev"))]
                extra_sl = [("ait", S.cur("ait"))]
            emit_phaseA_tile(k, 2, PA2, extra_mm, extra_ev, extra_sl)
        emit_phaseB(2)

        osw_all = S.cur("osw")
        S.emit("y", lambda e, n=osw_all: e.wait_ge(S.sems["osw"], n))
        break

    with nc.Block() as block:
        @block.gpsimd
        def _(eng):
            for f in S.q["g"]:
                f(eng)

        @block.sync
        def _(eng):
            for f in S.q["y"]:
                f(eng)

        @block.scalar
        def _(eng):
            for f in S.q["s"]:
                f(eng)

        @block.vector
        def _(eng):
            for f in S.q["v"]:
                f(eng)

        @block.tensor
        def _(eng):
            for f in S.q["t"]:
                f(eng)

    return nc


_NC_CACHE = [None]


def kernel(L0_attn_data, L0_attn_mask, L0_attn_weight, L0_attn_bias,
           L0_ffn_qw_gate, L0_ffn_qw_up, L0_ffn_qw_down,
           L0_ffn_s_gate, L0_ffn_z_gate, L0_ffn_s_up, L0_ffn_z_up,
           L0_ffn_s_down, L0_ffn_z_down):
    x = np.asarray(L0_attn_data, dtype=np.float32).reshape(NCORE * T, H)
    qgT = np.ascontiguousarray(np.asarray(L0_ffn_qw_gate, dtype=np.int32).T)
    quT = np.ascontiguousarray(np.asarray(L0_ffn_qw_up, dtype=np.int32).T)
    qdT = np.ascontiguousarray(np.asarray(L0_ffn_qw_down, dtype=np.int32).T)
    sgT = np.ascontiguousarray(np.asarray(L0_ffn_s_gate, dtype=np.float32).T)
    zgT = np.ascontiguousarray(np.asarray(L0_ffn_z_gate, dtype=np.float32).T)
    suT = np.ascontiguousarray(np.asarray(L0_ffn_s_up, dtype=np.float32).T)
    zuT = np.ascontiguousarray(np.asarray(L0_ffn_z_up, dtype=np.float32).T)
    sdT = np.ascontiguousarray(np.asarray(L0_ffn_s_down, dtype=np.float32).T)
    zdT = np.ascontiguousarray(np.asarray(L0_ffn_z_down, dtype=np.float32).T)
    maskA = np.ascontiguousarray(np.asarray(L0_attn_mask, dtype=np.int32))
    wT = np.ascontiguousarray(np.asarray(L0_attn_weight, dtype=np.float32).T)
    biasT = np.ascontiguousarray(np.asarray(L0_attn_bias, dtype=np.float32).reshape(32, 128).T)

    if _NC_CACHE[0] is None:
        _NC_CACHE[0] = build_nc()
    nc = _NC_CACHE[0]

    in_maps = []
    for c in range(NCORE):
        xTc = np.ascontiguousarray(x[c * T:(c + 1) * T, :].T)
        in_maps.append({
            "xT": xTc, "qgT": qgT, "quT": quT, "qdT": qdT,
            "sgT": sgT, "zgT": zgT, "suT": suT, "zuT": zuT, "sdT": sdT, "zdT": zdT,
            "mask": maskA, "wT": wT, "biasT": biasT,
        })
    res = run_bass_kernel_spmd(nc, in_maps, core_ids=list(range(NCORE)))
    outs = [np.asarray(res.results[c]["outT"]).T for c in range(NCORE)]
    return np.concatenate(outs, axis=0).reshape(4, 4096, 4096).astype(np.float32)


# version 06
# speedup vs baseline: 1.1909x; 1.0442x over previous
import numpy as np
import ml_dtypes
import concourse.bass as bass
import concourse.mybir as mybir
from concourse.bass_utils import run_bass_kernel_spmd

F32 = mybir.dt.float32
BF16 = mybir.dt.float16
I32 = mybir.dt.int32
AF = mybir.ActivationFunctionType
ALU = mybir.AluOpType
BF = np.float16

H = 4096
HP = 32
I = 11008
NI = 86
T = 2048
NTC = 4
TW = 512
NCORE = 8
CHG = [(0, 512), (512, 1920), (1920, 3328), (3328, 4736), (4736, 6144),
       (6144, 7552), (7552, 8960), (8960, 10368), (10368, 11008)]
CHD = [(0, 1408), (1408, 2816), (2816, 4096)]
CH_TILE_END = [e // 128 for (_, e) in CHG]


class Sch:

    def __init__(self, nc):
        self.nc = nc
        self.q = {"t": [], "v": [], "s": [], "y": [], "g": []}
        self.cnt = {}
        self.sems = {}

    def sem(self, name):
        if name not in self.sems:
            self.sems[name] = self.nc.alloc_semaphore(name)
            self.cnt[name] = 0
        return self.sems[name]

    def cur(self, name):
        self.sem(name)
        return self.cnt[name]

    def inc(self, name, amt):
        self.sem(name)
        self.cnt[name] += amt
        return self.cnt[name]

    def emit(self, qn, fn):
        self.q[qn].append(fn)


def build_nc(stage=99, phb_tc=NTC, phb_grp=8):
    nc = bass.Bass("TRN2", target_bir_lowering=False, detect_race_conditions=False)

    xT = nc.dram_tensor("xT", [H, T], F32, kind="ExternalInput").ap()
    qgT = nc.dram_tensor("qgT", [H, I], I32, kind="ExternalInput").ap()
    quT = nc.dram_tensor("quT", [H, I], I32, kind="ExternalInput").ap()
    qdT = nc.dram_tensor("qdT", [I, H], I32, kind="ExternalInput").ap()
    sgT = nc.dram_tensor("sgT", [32, I], F32, kind="ExternalInput").ap()
    zgT = nc.dram_tensor("zgT", [32, I], F32, kind="ExternalInput").ap()
    suT = nc.dram_tensor("suT", [32, I], F32, kind="ExternalInput").ap()
    zuT = nc.dram_tensor("zuT", [32, I], F32, kind="ExternalInput").ap()
    sdT = nc.dram_tensor("sdT", [NI, H], F32, kind="ExternalInput").ap()
    zdT = nc.dram_tensor("zdT", [NI, H], F32, kind="ExternalInput").ap()
    mask = nc.dram_tensor("mask", [H, H], I32, kind="ExternalInput").ap()
    wT = nc.dram_tensor("wT", [H, H], F32, kind="ExternalInput").ap()
    biasT = nc.dram_tensor("biasT", [128, 32], F32, kind="ExternalInput").ap()
    outT = nc.dram_tensor("outT", [H, T], F32, kind="ExternalOutput").ap()

    wg_s = nc.dram_tensor("wg_s", [H, I], BF16, kind="Internal").ap()
    wu_s = nc.dram_tensor("wu_s", [H, I], BF16, kind="Internal").ap()
    wd_s = nc.dram_tensor("wd_s", [I, H], BF16, kind="Internal").ap()
    sg_b = nc.dram_tensor("sg_b", [32, I], BF16, kind="Internal").ap()
    zg_b = nc.dram_tensor("zg_b", [32, I], BF16, kind="Internal").ap()
    su_b = nc.dram_tensor("su_b", [32, I], BF16, kind="Internal").ap()
    zu_b = nc.dram_tensor("zu_b", [32, I], BF16, kind="Internal").ap()
    sd_b = nc.dram_tensor("sd_b", [NI, H], BF16, kind="Internal").ap()
    zd_b = nc.dram_tensor("zd_b", [NI, H], BF16, kind="Internal").ap()
    hact_s = nc.dram_tensor("hact_s", [I, T], BF16, kind="Internal").ap()
    xcsp_s = nc.dram_tensor("xcsp_s", [H, T], BF16, kind="Internal").ap()
    xnew_s = nc.dram_tensor("xnew_s", [H, T], BF16, kind="Internal").ap()
    wm_s = nc.dram_tensor("wm_s", [H, H], BF16, kind="Internal").ap()
    bmh_s = nc.dram_tensor("bmh_s", [128, H], BF16, kind="Internal").ap()
    mrow_s = nc.dram_tensor("mrow_s", [2, T], F32, kind="Internal").ap()

    ones_col = nc.inline_tensor(np.ones((128, 1), dtype=BF), name="ones_col").ap()
    eps_c = nc.inline_tensor(np.full((128, 1), 1e-5, dtype=np.float32), name="eps_c").ap()
    negh_c = nc.inline_tensor(np.full((128, 1), -0.5, dtype=np.float32), name="negh_c").ap()
    b4_np = np.zeros((128, 4), dtype=np.float32)
    for n in range(128):
        b4_np[n, n // 32] = 1.0
    b4_c = nc.inline_tensor(b4_np, name="b4_c").ap()
    i128_c = nc.inline_tensor(np.eye(128, dtype=np.float32), name="i128_c").ap()
    e_np = np.zeros((128, H), dtype=BF)
    for n in range(H):
        e_np[n // 32, n] = 1.0
    e_c = nc.inline_tensor(e_np, name="e_c").ap()

    AR = nc.alloc_sbuf_tensor("AR", [128, HP * T], BF16)
    SLG = nc.alloc_sbuf_tensor("SLG", [128, 8192], BF16)
    SLU = nc.alloc_sbuf_tensor("SLU", [128, 8192], BF16)
    QS = nc.alloc_sbuf_tensor("QS", [128, 4096], BF16)
    SZB = nc.alloc_sbuf_tensor("SZB", [128, 5632], BF16)
    TMP = nc.alloc_sbuf_tensor("TMP", [128, 2048], BF16)
    GC = nc.alloc_sbuf_tensor("GC", [128, 4096], BF16)
    UC = nc.alloc_sbuf_tensor("UC", [128, 2048], BF16)
    RB = nc.alloc_sbuf_tensor("RB", [128, 2048], BF16)
    MB = nc.alloc_sbuf_tensor("MB", [128, 2048], BF16)
    BI = nc.alloc_sbuf_tensor("BI", [128, 32], F32)
    BMT = nc.alloc_sbuf_tensor("BMT", [128, 128], BF16)
    MRED = nc.alloc_sbuf_tensor("MRED", [128, 256], F32)
    SROW = nc.alloc_sbuf_tensor("SROW", [1, 512], F32)
    ONES = nc.alloc_sbuf_tensor("ONES", [128, 1], BF16)
    EPS = nc.alloc_sbuf_tensor("EPS", [128, 1], F32)
    NEGH = nc.alloc_sbuf_tensor("NEGH", [128, 1], F32)
    B4 = nc.alloc_sbuf_tensor("B4", [128, 4], F32)

    PSA = nc.alloc_psum_tensor("PSA", [128, 2048], F32)
    PSB = nc.alloc_psum_tensor("PSB", [128, 2048], F32)

    S = Sch(nc)

    def ar_x(hp, tc=None):
        if tc is None:
            return AR[:, hp * T:(hp + 1) * T]
        return AR[:, hp * T + tc * TW: hp * T + (tc + 1) * TW]

    def ar_h(it):
        return AR[:, it * TW:(it + 1) * TW]

    def ar_xc(hp):
        base = NI * TW
        return AR[:, base + hp * TW: base + (hp + 1) * TW]

    def pa_gu(tc):
        pg = PSA if tc % 2 == 0 else PSB
        off = 0 if tc < 2 else 1024
        return pg[:, off:off + 512], pg[:, off + 512:off + 1024]

    def emit_square(hp, wait_specs):
        S.inc("stq", 1)
        def f(e, hp=hp, ws=tuple(wait_specs), s=S.sem("stq")):
            for nm, val in ws:
                if val > 0:
                    e.wait_ge(S.sems[nm], val)
            e.activation(UC[:, :], ar_x(hp), AF.Square).then_inc(s, 1)
        S.emit("s", f)

    def emit_stats_mms(extra_wait):
        st0 = S.cur("stq") - HP
        for hp in range(HP):
            S.inc("stm", 1)
            def f(e, hp=hp, st0=st0, ew=tuple(extra_wait), s=S.sem("stm")):
                if hp == 0:
                    for nm, val in ew:
                        if val > 0:
                            e.wait_ge(S.sems[nm], val)
                e.wait_ge(S.sems["stq"], st0 + hp + 1)
                for tc in range(NTC):
                    e.matmul(PSA[0:1, tc * TW:(tc + 1) * TW], ONES[:, :],
                             ar_x(hp, tc), start=(hp == 0), stop=(hp == HP - 1))
                last = None
                for tc in range(NTC):
                    last = e.matmul(PSB[0:1, tc * TW:(tc + 1) * TW], ONES[:, :],
                                    UC[:, tc * TW:(tc + 1) * TW],
                                    start=(hp == 0), stop=(hp == HP - 1))
                last.then_inc(s, 1)
            S.emit("t", f)

    def emit_stats_tail(sub):
        stm_n = S.cur("stm")
        bb_prev = S.cur("bb")
        MRW0 = S.cur("mrw")
        for tc in range(NTC):
            strA = S.inc("str", 1)
            S.inc("mrw", 16)
            mrwA = S.cur("mrw")
            S.inc("srr", 1)
            srrA = S.cur("srr")
            S.inc("str", 1)
            strB = S.cur("str")
            S.inc("mrw", 16)
            last_tc = tc == NTC - 1
            if last_tc:
                S.inc("ste", 1)
            def fv1(e, tc=tc, stm_n=stm_n, strA=strA, mrwA=mrwA, MRW0=MRW0,
                    s_str=S.sem("str"), s_srr=S.sem("srr")):
                if tc == 0:
                    e.wait_ge(S.sems["stm"], stm_n)
                else:
                    e.wait_ge(S.sems["mrw"], MRW0 + 32 * tc)
                e.tensor_single_scalar(SROW[0:1, :], PSA[0:1, tc * TW:(tc + 1) * TW],
                                       1.0 / H, ALU.mult).then_inc(s_str, 1)
                e.wait_ge(S.sems["mrw"], mrwA)
                e.tensor_mul(SROW[0:1, :], SROW[0:1, :], SROW[0:1, :])
                ins = e.scalar_tensor_tensor(SROW[0:1, :], PSB[0:1, tc * TW:(tc + 1) * TW],
                                             1.0 / H, SROW[0:1, :], ALU.mult, ALU.subtract)
                ins.then_inc(s_srr, 1)
            S.emit("v", fv1)
            def fs1(e, tc=tc, strA=strA, srrA=srrA, strB=strB, bb_prev=bb_prev,
                    last_tc=last_tc, s_mrw=S.sem("mrw"), s_str=S.sem("str"),
                    s_ste=S.sem("ste")):
                e.wait_ge(S.sems["str"], strA)
                if bb_prev > 0 and tc == 0:
                    e.wait_ge(S.sems["bb"], bb_prev)
                e.dma_start(out=mrow_s[0:1, tc * TW:(tc + 1) * TW],
                            in_=SROW[0:1, :]).then_inc(s_mrw, 16)
                e.wait_ge(S.sems["srr"], srrA)
                if last_tc:
                    e.sem_inc(s_ste, 1)
                e.activation(SROW[0:1, :], SROW[0:1, :], AF.Ln, bias=EPS[0:1, :])
                e.activation(SROW[0:1, :], SROW[0:1, :], AF.Exp,
                             scale=NEGH[0:1, :]).then_inc(s_str, 1)
                e.wait_ge(S.sems["str"], strB)
                e.dma_start(out=mrow_s[1:2, tc * TW:(tc + 1) * TW],
                            in_=SROW[0:1, :]).then_inc(s_mrw, 16)
            S.emit("s", fs1)
        mrw_n = S.cur("mrw")
        consumer_waits = []
        if sub == 2:
            consumer_waits = [("ev", S.cur("ev")), ("yev", S.cur("yev")), ("ctr", S.cur("ctr"))]
        elif sub == 3:
            consumer_waits = [("aev", S.cur("aev")), ("ctr", S.cur("ctr"))]
        S.inc("bb", 32)
        def fg(e, mrw_n=mrw_n, cw=tuple(consumer_waits), s=S.sem("bb")):
            e.wait_ge(S.sems["mrw"], mrw_n)
            for nm, val in cw:
                if val > 0:
                    e.wait_ge(S.sems[nm], val)
            e.dma_start(out=MB[:, :], in_=mrow_s[0:1, :].broadcast_to([128, T])).then_inc(s, 16)
            e.dma_start(out=RB[:, :], in_=mrow_s[1:2, :].broadcast_to([128, T])).then_inc(s, 16)
        S.emit("g", fg)
        bb_n = S.cur("bb")
        for hp in range(HP):
            S.inc("ctr", 1)
            def fc(e, hp=hp, bb_n=bb_n, stm_n=stm_n, s=S.sem("ctr")):
                if hp == 0:
                    e.wait_ge(S.sems["bb"], bb_n)
                    e.wait_ge(S.sems["stm"], stm_n)
                e.tensor_sub(ar_x(hp), ar_x(hp), MB[:, :]).then_inc(s, 1)
            S.emit("v", fc)

    def emit_stats_full(sub, ready_waits, psum_free_waits):
        stq0 = S.cur("stq")
        stm0 = S.cur("stm")
        for hp in range(HP):
            ws = list(ready_waits) if hp == 0 else []
            if hp >= 1:
                ws.append(("stm", stm0 + hp))
            emit_square(hp, ws)
        emit_stats_mms(psum_free_waits)
        emit_stats_tail(sub)

    dq_state = {"count": 0, "slot_writer": [None, None], "pending_fw": None}

    def emit_dq_iter(qsrc, s_bf, z_bf, wdst, grp, c0, c1, wbase):
        it_par = dq_state["count"] % 2
        dq_state["count"] += 1
        csz = c1 - c0
        qs_v = QS[:, it_par * 2048: it_par * 2048 + csz]
        sb_v = SZB[:, it_par * 2816: it_par * 2816 + csz]
        zb_v = SZB[:, it_par * 2816 + 1408: it_par * 2816 + 1408 + csz]
        n_dqv_gate = S.cur("dqv") - 2 if dq_state["count"] >= 2 else 0
        prev_writer = dq_state["slot_writer"][it_par]
        qsem = "qld%d" % it_par
        ssem = "bls%d" % it_par
        zsem = "blz%d" % it_par
        wsem = wbase + str(it_par)
        S.inc(qsem, 16)
        def fq(e, qsrc=qsrc, grp=grp, c0=c0, c1=c1, qs_v=qs_v, pw=prev_writer,
               s=S.sem(qsem)):
            if pw is not None:
                e.wait_ge(S.sems[pw[0]], pw[1])
            e.dma_start(out=qs_v, in_=qsrc[grp * 128:(grp + 1) * 128, c0:c1]).then_inc(s, 16)
        S.emit("g", fq)
        if dq_state["pending_fw"] is not None:
            S.emit("g", dq_state["pending_fw"])
            dq_state["pending_fw"] = None
        S.inc(ssem, 16)
        def fbs(e, s_bf=s_bf, grp=grp, c0=c0, c1=c1, sb_v=sb_v, nv=n_dqv_gate,
                s=S.sem(ssem)):
            if nv > 0:
                e.wait_ge(S.sems["dqv"], nv)
            e.dma_start(out=sb_v, in_=s_bf[grp:grp + 1, c0:c1].broadcast_to([128, c1 - c0])).then_inc(s, 16)
        S.emit("s", fbs)
        S.inc(zsem, 16)
        def fbz(e, z_bf=z_bf, grp=grp, c0=c0, c1=c1, zb_v=zb_v, nv=n_dqv_gate,
                s=S.sem(zsem)):
            if nv > 0:
                e.wait_ge(S.sems["dqv"], nv)
            e.dma_start(out=zb_v, in_=z_bf[grp:grp + 1, c0:c1].broadcast_to([128, c1 - c0])).then_inc(s, 16)
        S.emit("s", fbz)
        S.inc("dqv", 2)
        nq = S.cur(qsem)
        ns = S.cur(ssem)
        nz = S.cur(zsem)
        def fv(e, qs_v=qs_v, sb_v=sb_v, zb_v=zb_v, nq=nq, ns=ns, nz=nz, qsem=qsem,
               ssem=ssem, zsem=zsem, s=S.sem("dqv")):
            e.wait_ge(S.sems[qsem], nq)
            e.wait_ge(S.sems[ssem], ns)
            e.wait_ge(S.sems[zsem], nz)
            e.tensor_sub(qs_v, qs_v, zb_v).then_inc(s, 1)
            e.tensor_mul(qs_v, qs_v, sb_v).then_inc(s, 1)
        S.emit("v", fv)
        nv2 = S.cur("dqv")
        S.inc(wsem, 16)
        dq_state["slot_writer"][it_par] = (wsem, S.cur(wsem))
        def fw(e, wdst=wdst, grp=grp, c0=c0, c1=c1, qs_v=qs_v, nv=nv2, s=S.sem(wsem)):
            e.wait_ge(S.sems["dqv"], nv)
            e.dma_start(out=wdst[grp * 128:(grp + 1) * 128, c0:c1], in_=qs_v).then_inc(s, 16)
        dq_state["pending_fw"] = fw

    def dq_worklist():
        work = []
        for (c0, c1) in CHG:
            for qsrc, s_bf, z_bf, wdst in ((qgT, sg_b, zg_b, wg_s), (quT, su_b, zu_b, wu_s)):
                for grp in range(32):
                    work.append(("gu", (qsrc, s_bf, z_bf, wdst, grp, c0, c1, "dqw")))
            work.append(("ms", None))
        for gi in range(NI):
            for (c0, c1) in CHD:
                work.append(("dn", (qdT, sd_b, zd_b, wd_s, gi, c0, c1, "dqd")))
        for kp in range(HP):
            work.append(("wm", kp))
        return work

    def emit_dq_chunk_milestone():
        if dq_state["pending_fw"] is not None:
            S.emit("g", dq_state["pending_fw"])
            dq_state["pending_fw"] = None
        w0 = S.cur("dqw0")
        w1 = S.cur("dqw1")
        S.inc("dqc", 1)
        def fms(e, w0=w0, w1=w1, s=S.sem("dqc")):
            if w0 > 0:
                e.wait_ge(S.sems["dqw0"], w0)
            if w1 > 0:
                e.wait_ge(S.sems["dqw1"], w1)
            e.sem_inc(s, 1)
        S.emit("g", fms)

    def emit_phaseA_tile(k, p, bases, extra_mm_waits=(), extra_ev_waits=(),
                         extra_sl_waits=()):
        MM0, EV0, HS0, HV0, HA0 = bases
        buf = k % 2
        wg3 = wg_s.rearrange("(hp p) i -> p hp i", p=128)
        wu3 = wu_s.rearrange("(hp p) i -> p hp i", p=128)
        if p == 1:
            ci = 0
            while CH_TILE_END[ci] <= k:
                ci += 1
            dqw_w = [("dqc", ci + 1)]
        else:
            dqw_w = []
        mm_n = MM0 + 4 * (k - 1) if k >= 2 else 0
        slsem = "sl%d" % buf
        for src3, dstT in ((wg3, SLG), (wu3, SLU)):
            S.inc(slsem, 16)
            def fl(e, k=k, buf=buf, src3=src3, dstT=dstT, dqw_w=tuple(dqw_w), mm_n=mm_n,
                   esw=tuple(extra_sl_waits), s=S.sem(slsem)):
                for nm, val in esw:
                    if val > 0:
                        e.wait_ge(S.sems[nm], val)
                for nm, val in dqw_w:
                    if val > 0:
                        e.wait_ge(S.sems[nm], val)
                if mm_n:
                    e.wait_ge(S.sems["mm"], mm_n)
                dst = dstT[:, buf * 4096:(buf + 1) * 4096].rearrange("p (hp i) -> p hp i", hp=HP)
                e.dma_start(out=dst, in_=src3[:, :, k * 128:(k + 1) * 128]).then_inc(s, 16)
            S.emit("y", fl)
        sl_n = S.cur(slsem)
        for tc in range(NTC):
            S.inc("mm", 1)
            def ft(e, k=k, tc=tc, buf=buf, sl_n=sl_n, EV0=EV0, emw=tuple(extra_mm_waits),
                   s=S.sem("mm")):
                if tc == 0:
                    e.wait_ge(S.sems["sl%d" % buf], sl_n)
                    if k == 0:
                        for nm, val in emw:
                            if val > 0:
                                e.wait_ge(S.sems[nm], val)
                if k > 0:
                    e.wait_ge(S.sems["ev"], EV0 + 8 * (k - 1) + 2 * tc + 2)
                gs, us = pa_gu(tc)
                for hp in range(HP):
                    lg = SLG[:, buf * 4096 + hp * 128: buf * 4096 + (hp + 1) * 128]
                    lu = SLU[:, buf * 4096 + hp * 128: buf * 4096 + (hp + 1) * 128]
                    rx = ar_x(hp, tc)
                    e.matmul(gs, lg, rx, start=(hp == 0), stop=(hp == HP - 1))
                    mmi = e.matmul(us, lu, rx, start=(hp == 0), stop=(hp == HP - 1))
                    if hp == HP - 1:
                        mmi.then_inc(s, 1)
            S.emit("t", ft)
        hs_list = []
        for tc in range(NTC):
            S.inc("ev", 2)
        for tc in range(NTC):
            S.inc("hs", 1)
            hs_list.append(S.cur("hs"))
        for tc in range(NTC):
            S.inc("hv", 1)
        def fv(e, k=k, p=p, buf=buf, MM0=MM0, HA0=HA0, hs_list=tuple(hs_list),
               eew=tuple(extra_ev_waits), s_ev=S.sem("ev"), s_hv=S.sem("hv")):
            for tc in range(NTC):
                e.wait_ge(S.sems["mm"], MM0 + 4 * k + tc + 1)
                if tc == 0:
                    if k >= 2:
                        e.wait_ge(S.sems["ha%d" % buf], HA0[buf] + 16 * ((k - 2) // 2 + 1))
                    if k == 0:
                        for nm, val in eew:
                            if val > 0:
                                e.wait_ge(S.sems[nm], val)
                gs, us = pa_gu(tc)
                gslot = GC[:, buf * 2048 + tc * TW: buf * 2048 + (tc + 1) * TW]
                e.tensor_mul(gslot, gs, RB[:, tc * TW:(tc + 1) * TW]).then_inc(s_ev, 1)
                e.tensor_mul(UC[:, tc * TW:(tc + 1) * TW], us,
                             RB[:, tc * TW:(tc + 1) * TW]).then_inc(s_ev, 1)
            for tc in range(NTC):
                e.wait_ge(S.sems["hs"], hs_list[tc])
                gslot = GC[:, buf * 2048 + tc * TW: buf * 2048 + (tc + 1) * TW]
                e.tensor_mul(gslot, gslot, UC[:, tc * TW:(tc + 1) * TW]).then_inc(s_hv, 1)
        S.emit("v", fv)
        S.inc("ha%d" % buf, 16)
        def fsc(e, k=k, buf=buf, EV0=EV0, HV0=HV0, s_hs=S.sem("hs"), s_ha=S.sem("ha%d" % buf)):
            for tc in range(NTC):
                e.wait_ge(S.sems["ev"], EV0 + 8 * k + 2 * tc + 1)
                gslot = GC[:, buf * 2048 + tc * TW: buf * 2048 + (tc + 1) * TW]
                e.activation(gslot, gslot, AF.Silu).then_inc(s_hs, 1)
            e.wait_ge(S.sems["hv"], HV0 + 4 * k + 4)
            e.dma_start(out=hact_s[k * 128:(k + 1) * 128, :],
                        in_=GC[:, buf * 2048:(buf + 1) * 2048]).then_inc(s_ha, 16)
        S.emit("s", fsc)

    def emit_xc_spill(p):
        ctr_n = S.cur("ctr")
        xcr_prev = S.cur("xcr")
        for hp in range(HP):
            S.inc("xcw", 16)
            def f(e, hp=hp, ctr_n=ctr_n, xcr_prev=xcr_prev, s=S.sem("xcw")):
                if hp == 0:
                    e.wait_ge(S.sems["ctr"], ctr_n)
                    if xcr_prev > 0:
                        e.wait_ge(S.sems["xcr"], xcr_prev)
                e.dma_start(out=xcsp_s[hp * 128:(hp + 1) * 128, :], in_=ar_x(hp)).then_inc(s, 16)
            S.emit("s", f)

    def emit_phaseB(p, n_tc=NTC, n_grp=8):
        MM_all = S.cur("mm")
        XCW_all = S.cur("xcw")
        HA_all = (S.cur("ha0"), S.cur("ha1"))
        DQD_all = (S.cur("dqd0"), S.cur("dqd1"))
        HCL0 = S.cur("hcl")
        XCR0 = S.cur("xcr")

        YIT0 = S.cur("yit")

        YEV0 = S.cur("yev")
        wsem = "xnw" if p == 1 else "osw"
        XNW0 = S.cur(wsem)
        wdsem = "wdl%d" % p
        WDC0 = S.cur("wdc%d" % p)
        EV_all = S.cur("ev")
        AEV_all = S.cur("aev")
        STE_all = S.cur("ste")

        def out_slot(par, j, p):
            if p == 1:
                return GC[:, par * 2048 + j * 512: par * 2048 + (j + 1) * 512]
            if par == 0:
                return GC[:, :].bitcast(F32)[:, j * 512:(j + 1) * 512]
            if j < 2:
                return UC[:, :].bitcast(F32)[:, j * 512:(j + 1) * 512]
            return TMP[:, :].bitcast(F32)[:, (j - 2) * 512:(j - 1) * 512]

        for tc in range(n_tc):
            yit_w = YIT0 + 88 * tc
            yev_w = YEV0 + 32 * tc
            for it in range(NI):
                S.inc("hcl", 16)
                def fh(e, it=it, tc=tc, yit_w=yit_w, yev_w=yev_w, MM_all=MM_all,
                       XCW_all=XCW_all, HA_all=HA_all, s=S.sem("hcl")):
                    if it == 0:
                        if tc == 0:
                            e.wait_ge(S.sems["mm"], MM_all)
                            e.wait_ge(S.sems["xcw"], XCW_all)
                            e.wait_ge(S.sems["ha0"], HA_all[0])
                            e.wait_ge(S.sems["ha1"], HA_all[1])
                        else:
                            e.wait_ge(S.sems["yit"], yit_w)
                            e.wait_ge(S.sems["yev"], yev_w)
                    e.dma_start(out=ar_h(it),
                                in_=hact_s[it * 128:(it + 1) * 128, tc * TW:(tc + 1) * TW]).then_inc(s, 16)
                S.emit("g", fh)
            for hp in range(HP):
                S.inc("xcr", 16)
                def fx(e, hp=hp, tc=tc, s=S.sem("xcr")):
                    e.dma_start(out=ar_xc(hp),
                                in_=xcsp_s[hp * 128:(hp + 1) * 128, tc * TW:(tc + 1) * TW]).then_inc(s, 16)
                S.emit("g", fx)
            hcl_n = S.cur("hcl")
            xcr_n = S.cur("xcr")

            for grp in range(n_grp):
                gidx = tc * 8 + grp
                for it in range(NI):
                    ridx = gidx * NI + it
                    slot = ridx % 16
                    S.inc(wdsem, 16)
                    def fw(e, it=it, grp=grp, slot=slot, ridx=ridx, p=p,
                           YIT0=YIT0, DQD_all=DQD_all, MM_all=MM_all, wdsem=wdsem,
                           s=S.sem(wdsem)):
                        if ridx == 0:
                            if p == 1:
                                e.wait_ge(S.sems["dqd0"], DQD_all[0])
                                e.wait_ge(S.sems["dqd1"], DQD_all[1])
                            e.wait_ge(S.sems["mm"], MM_all)
                        if ridx >= 16:
                            g2, it2 = divmod(ridx - 16, NI)
                            e.wait_ge(S.sems["yit"], YIT0 + g2 * 11 + it2 // 8 + 1)
                        e.dma_start(out=SLG[:, slot * 512:(slot + 1) * 512],
                                    in_=wd_s[it * 128:(it + 1) * 128,
                                             grp * 512:(grp + 1) * 512]).then_inc(s, 16)
                    S.emit("y", fw)
                    if it % 8 == 7 or it == NI - 1:
                        cl_end = gidx * NI + min((it // 8 + 1) * 8, NI)
                        S.inc("wdc%d" % p, 1)
                        def fm(e, n16=16 * cl_end, wdsem=wdsem,
                               s=S.sem("wdc%d" % p)):
                            e.wait_ge(S.sems[wdsem], n16)
                            e.sem_inc(s, 1)
                        S.emit("y", fm)
                for it in range(NI):
                    if it % 8 == 7 or it == NI - 1:
                        S.inc("yit", 1)
                def ft(e, tc=tc, grp=grp, gidx=gidx, p=p, hcl_n=hcl_n,
                       wdsem=wdsem, WDC0=WDC0, YEV0=YEV0, EV_all=EV_all, AEV_all=AEV_all,
                       STE_all=STE_all, s_yit=S.sem("yit")):
                    pg = PSA if gidx % 2 == 0 else PSB
                    if grp == 0:
                        e.wait_ge(S.sems["hcl"], hcl_n)
                    if gidx < 2:
                        if p == 1:
                            e.wait_ge(S.sems["ev"], EV_all)
                        else:
                            e.wait_ge(S.sems["aev"], AEV_all)
                        e.wait_ge(S.sems["ste"], STE_all)
                    else:
                        e.wait_ge(S.sems["yev"], YEV0 + 4 * (gidx - 1))
                    for it in range(NI):
                        ridx = gidx * NI + it
                        if it % 8 == 0:
                            e.wait_ge(S.sems["wdc%d" % p], WDC0 + gidx * 11 + it // 8 + 1)
                        slot = ridx % 16
                        last = None
                        for j in range(4):
                            last = e.matmul(pg[:, j * 512:(j + 1) * 512],
                                            SLG[:, slot * 512 + j * 128: slot * 512 + (j + 1) * 128],
                                            ar_h(it), start=(it == 0), stop=(it == NI - 1))
                        if it % 8 == 7 or it == NI - 1:
                            last.then_inc(s_yit, 1)
                S.emit("t", ft)
                yit_n = S.cur("yit")
                for j in range(4):
                    S.inc("yev", 1)
                yev_after = S.cur("yev")
                def fv(e, tc=tc, grp=grp, gidx=gidx, p=p, yit_n=yit_n, xcr_n=xcr_n,
                       HA_all=HA_all, XNW0=XNW0, wsem=wsem, s=S.sem("yev")):
                    pg = PSA if gidx % 2 == 0 else PSB
                    e.wait_ge(S.sems["yit"], yit_n)
                    if gidx == 0:
                        e.wait_ge(S.sems["xcr"], xcr_n)
                        e.wait_ge(S.sems["ha0"], HA_all[0])
                        e.wait_ge(S.sems["ha1"], HA_all[1])
                    elif grp == 0:
                        e.wait_ge(S.sems["xcr"], xcr_n)
                    if gidx >= 2:
                        e.wait_ge(S.sems[wsem], XNW0 + 16 * 4 * (gidx - 1))
                    for j in range(4):
                        hp = grp * 4 + j
                        slot = out_slot(gidx % 2, j, p)
                        e.scalar_tensor_tensor(slot, pg[:, j * 512:(j + 1) * 512], 0.5,
                                               ar_xc(hp), ALU.mult, ALU.add)
                        e.tensor_add(slot, slot, MB[:, tc * TW:(tc + 1) * TW]).then_inc(s, 1)
                S.emit("v", fv)
                for j in range(4):
                    hp = grp * 4 + j
                    S.inc(wsem, 16)
                    def fst(e, tc=tc, gidx=gidx, j=j, hp=hp, p=p, yev_after=yev_after,
                            s=S.sem(wsem)):
                        e.wait_ge(S.sems["yev"], yev_after - 3 + j)
                        slot = out_slot(gidx % 2, j, p)
                        if p == 1:
                            dst = xnew_s[hp * 128:(hp + 1) * 128, tc * TW:(tc + 1) * TW]
                        else:
                            dst = outT[hp * 128:(hp + 1) * 128, tc * TW:(tc + 1) * TW]
                        e.dma_start(out=dst, in_=slot).then_inc(s, 16)
                    S.emit("s", fst)

    def emit_wm_piece(kp):
        dq_all = [(nm, S.cur(nm)) for nm in ("dqw0", "dqw1", "dqd0", "dqd1")]
        bhw_all = S.cur("bhw")
        dqv_all = S.cur("dqv")
        if True:
            S.inc("wtl", 16)
            def fg(e, kp=kp, dq_all=tuple(dq_all), dqv_all=dqv_all, s=S.sem("wtl")):
                if kp == 0:
                    for nm, val in dq_all:
                        if val > 0:
                            e.wait_ge(S.sems[nm], val)
                    e.wait_ge(S.sems["dqv"], dqv_all)
                else:
                    e.wait_ge(S.sems["wmv"], 2 * kp)
                e.dma_start(out=QS[:, :], in_=wT[kp * 128:(kp + 1) * 128, :]).then_inc(s, 16)
            S.emit("g", fg)
            S.inc("wte", 16 * 4)
            def fge(e, kp=kp, bhw_all=bhw_all, dqv_all=dqv_all, s=S.sem("wte")):
                if kp == 0:
                    e.wait_ge(S.sems["bhw"], bhw_all)
                    e.wait_ge(S.sems["dqv"], dqv_all)
                else:
                    e.wait_ge(S.sems["wmv"], 2 * kp)
                for j in range(4):
                    kb = kp * 4 + j
                    e.dma_start(out=SZB[32 * j:32 * (j + 1), 0:4096],
                                in_=bmh_s[kb:kb + 1, :].broadcast_to([32, 4096])).then_inc(s, 16)
            S.emit("s", fge)
            wtl_n = S.cur("wtl")
            wte_n = S.cur("wte")
            for h in range(2):
                S.inc("wmv", 1)
                wmb_need = 16 * (2 * kp + h)
                def fv(e, kp=kp, h=h, wtl_n=wtl_n, wte_n=wte_n, wmb_need=wmb_need,
                       s=S.sem("wmv")):
                    if h == 0:
                        e.wait_ge(S.sems["wtl"], wtl_n)
                        e.wait_ge(S.sems["wte"], wte_n)
                    if wmb_need > 0:
                        e.wait_ge(S.sems["wmb"], wmb_need)
                    e.tensor_mul(TMP[:, :], QS[:, h * 2048:(h + 1) * 2048],
                                 SZB[:, h * 2048:(h + 1) * 2048]).then_inc(s, 1)
                S.emit("v", fv)
                wmv_n = S.cur("wmv")
                S.inc("wmb", 16)
                def fs2(e, kp=kp, h=h, wmv_n=wmv_n, s=S.sem("wmb")):
                    e.wait_ge(S.sems["wmv"], wmv_n)
                    e.dma_start(out=wm_s[kp * 128:(kp + 1) * 128, h * 2048:(h + 1) * 2048],
                                in_=TMP[:, :]).then_inc(s, 16)
                S.emit("s", fs2)

    def emit_attn():
        XRL_all = S.cur("xrl")
        WMB_all = S.cur("wmb")
        YIT_all = S.cur("yit")
        YEV_all = S.cur("yev")
        BB_all = S.cur("bb")
        CTR_all = S.cur("ctr")
        AWL0 = S.cur("awl")
        AWC0 = S.cur("awc")
        AIT0 = S.cur("ait")
        AEV0 = S.cur("aev")
        XNW_all = S.cur("xnw")
        STE_all = S.cur("ste")

        def ctmp(kp):
            tens = SLG if kp < 16 else SLU
            return tens[:, (kp % 16) * 512:(kp % 16 + 1) * 512]

        for tc in range(NTC):
            ait_w = AIT0 + 64 * tc
            for kp in range(HP):
                S.inc("atc", 1)
            atc_n = S.cur("atc")
            def fcp(e, tc=tc, ait_w=ait_w, XRL_all=XRL_all, YIT_all=YIT_all,
                    CTR_all=CTR_all, YEV_all=YEV_all, s=S.sem("atc")):
                if tc == 0:
                    e.wait_ge(S.sems["xrl"], XRL_all)
                    e.wait_ge(S.sems["yit"], YIT_all)
                    e.wait_ge(S.sems["yev"], YEV_all)
                    e.wait_ge(S.sems["ctr"], CTR_all)
                else:
                    e.wait_ge(S.sems["ait"], ait_w)
                for kp in range(HP):
                    e.tensor_copy(ctmp(kp), ar_x(kp, tc)).then_inc(s, 1)
            S.emit("v", fcp)

            for grp in range(8):
                gidx = tc * 8 + grp
                for kp in range(HP):
                    ridx = gidx * HP + kp
                    slot = ridx % 11
                    S.inc("awl", 16)
                    def fw(e, kp=kp, grp=grp, slot=slot, ridx=ridx, WMB_all=WMB_all,
                           AIT0=AIT0, s=S.sem("awl")):
                        if ridx == 0:
                            e.wait_ge(S.sems["wmb"], WMB_all)
                            e.wait_ge(S.sems["wmv"], 2 * HP)
                        if ridx >= 11:
                            g2, kp2 = divmod(ridx - 11, HP)
                            e.wait_ge(S.sems["ait"], AIT0 + g2 * 8 + kp2 // 4 + 1)
                        e.dma_start(out=SZB[:, slot * 512:(slot + 1) * 512],
                                    in_=wm_s[kp * 128:(kp + 1) * 128,
                                             grp * 512:(grp + 1) * 512]).then_inc(s, 16)
                    S.emit("y", fw)
                    if kp % 4 == 3:
                        cl_end = gidx * HP + (kp // 4 + 1) * 4
                        S.inc("awc", 1)
                        def fm(e, n16=16 * cl_end, s=S.sem("awc")):
                            e.wait_ge(S.sems["awl"], n16)
                            e.sem_inc(s, 1)
                        S.emit("y", fm)
                for kp in range(HP):
                    if kp % 4 == 3:
                        S.inc("ait", 1)
                def ft(e, tc=tc, grp=grp, gidx=gidx, atc_n=atc_n, AWC0=AWC0,
                       AEV0=AEV0, YEV_all=YEV_all, STE_all=STE_all,
                       s_ait=S.sem("ait")):
                    pg = PSA if gidx % 2 == 0 else PSB
                    if grp == 0:
                        e.wait_ge(S.sems["atc"], atc_n)
                    if gidx < 2:
                        e.wait_ge(S.sems["yev"], YEV_all)
                        e.wait_ge(S.sems["ste"], STE_all)
                    else:
                        e.wait_ge(S.sems["aev"], AEV0 + 4 * (gidx - 1))
                    for kp in range(HP):
                        ridx = gidx * HP + kp
                        if kp % 4 == 0:
                            e.wait_ge(S.sems["awc"], AWC0 + gidx * 8 + kp // 4 + 1)
                        slot = ridx % 11
                        last = None
                        for j in range(4):
                            last = e.matmul(pg[:, j * 512:(j + 1) * 512],
                                            SZB[:, slot * 512 + j * 128: slot * 512 + (j + 1) * 128],
                                            ctmp(kp), start=(kp == 0), stop=(kp == HP - 1))
                        if kp % 4 == 3:
                            last.then_inc(s_ait, 1)
                S.emit("t", ft)
                ait_n = S.cur("ait")
                for j in range(4):
                    S.inc("aev", 1)
                def fv(e, tc=tc, grp=grp, gidx=gidx, ait_n=ait_n, BB_all=BB_all,
                       XNW_all=XNW_all, s=S.sem("aev")):
                    pg = PSA if gidx % 2 == 0 else PSB
                    e.wait_ge(S.sems["ait"], ait_n)
                    if gidx == 0:
                        e.wait_ge(S.sems["bb"], BB_all)
                        e.wait_ge(S.sems["xnw"], XNW_all)
                    for j in range(4):
                        nt = grp * 4 + j
                        slot = GC[:, (gidx % 2) * 2048 + j * 512: (gidx % 2) * 2048 + (j + 1) * 512]
                        e.tensor_mul(slot, pg[:, j * 512:(j + 1) * 512], RB[:, tc * TW:(tc + 1) * TW])
                        e.scalar_tensor_tensor(slot, slot, BI[:, nt:nt + 1], ar_x(nt, tc),
                                               ALU.add, ALU.add)
                        e.tensor_add(ar_x(nt, tc), slot, MB[:, tc * TW:(tc + 1) * TW]).then_inc(s, 1)
                S.emit("v", fv)

    def emit_bm_vec():
        S.inc("bmv", 1)
        def f1(e, s=S.sem("bmv")):
            e.wait_ge(S.sems["bmm"], HP)
            e.tensor_single_scalar(MRED[:, 0:128], PSB[:, 0:128], 0.0, ALU.is_gt).then_inc(s, 1)
        S.emit("v", f1)
        S.inc("bmc", 1)
        def f2(e, s=S.sem("bmc")):
            e.wait_ge(S.sems["bmt"], 1)
            e.tensor_copy(BMT[:, :], PSB[:, 512:640]).then_inc(s, 1)
        S.emit("v", f2)
        for cc in range(8):
            S.inc("bhe", 1)
            def f3(e, cc=cc, s=S.sem("bhe")):
                e.wait_ge(S.sems["bhm"], cc + 1)
                if cc == 0:
                    e.wait_ge(S.sems["stm"], HP)
                if cc >= 4:
                    e.wait_ge(S.sems["bhw"], 16 * cc)
                e.tensor_copy(UC[:, (cc % 4) * 512:(cc % 4 + 1) * 512],
                              PSA[:, 1024 + (cc % 2) * 512: 1536 + (cc % 2) * 512]).then_inc(s, 1)
            S.emit("v", f3)
        for cc in range(8):
            S.inc("bhw", 16)
            def f4(e, cc=cc, s=S.sem("bhw")):
                e.wait_ge(S.sems["bhe"], cc + 1)
                e.dma_start(out=bmh_s[:, cc * 512:(cc + 1) * 512],
                            in_=UC[:, (cc % 4) * 512:(cc % 4 + 1) * 512]).then_inc(s, 16)
            S.emit("s", f4)

    for src, dst in ((sgT, sg_b), (zgT, zg_b), (suT, su_b), (zuT, zu_b),
                     (sdT, sd_b), (zdT, zd_b)):
        S.inc("szc", 16)
        S.emit("g", lambda e, a=src, b=dst, s=S.sem("szc"): e.dma_start(out=b, in_=a).then_inc(s, 16))
    for hp in range(HP):
        S.inc("xl", 16)
        S.emit("g", lambda e, hp=hp, s=S.sem("xl"):
               e.dma_start(out=ar_x(hp), in_=xT[hp * 128:(hp + 1) * 128, :]).then_inc(s, 16))

    for src, dst in ((ones_col, ONES[:, :]), (eps_c, EPS[:, :]), (negh_c, NEGH[:, :]),
                     (b4_c, B4[:, :]), (e_c, GC[:, :])):
        S.inc("cst", 16)
        S.emit("y", lambda e, a=src, b=dst, s=S.sem("cst"): e.dma_start(out=b, in_=a).then_inc(s, 16))
    S.inc("cst", 16)
    S.emit("y", lambda e, s=S.sem("cst"): e.dma_start(out=BI[:, :], in_=biasT).then_inc(s, 16))
    N_CST = S.cur("cst")

    def emit_mask_load(j):
        dst = (SLG if j % 2 == 0 else SLU)[:, :].bitcast(I32)
        msem = "mkl%d" % (j % 2)
        S.inc(msem, 16)
        def f(e, j=j, dst=dst, s=S.sem(msem)):
            if j >= 2:
                e.wait_ge(S.sems["mkr"], j - 1)
            e.dma_start(out=dst, in_=mask[j * 128:(j + 1) * 128, :]).then_inc(s, 16)
        S.emit("s", f)

    for j in range(HP):
        emit_square(j, ([("xl", 16 * HP)] if j == 0 else []) + ([("stm", j)] if j >= 1 else []))

    emit_stats_mms([("cst", N_CST)])
    emit_stats_tail(1)
    for j in range(HP):
        emit_mask_load(j)
    for j in range(HP):
        S.inc("bmm", 1)
        def fbm(e, j=j, ste_n=S.cur("ste"), s=S.sem("bmm")):
            if j == 0:
                e.wait_ge(S.sems["ste"], ste_n)
            e.wait_ge(S.sems["mkr"], j + 1)
            e.matmul(PSB[:, 4 * j:4 * j + 4], MRED[:, (j % 2) * 128:(j % 2) * 128 + 128],
                     B4[:, :], start=True, stop=True).then_inc(s, 1)
        S.emit("t", fbm)
    S.inc("cst", 16)
    S.emit("y", lambda e, s=S.sem("cst"): (e.wait_ge(S.sems["bmm"], HP),
                                           e.dma_start(out=MRED[:, 128:256], in_=i128_c).then_inc(s, 16)))
    CST_I128 = S.cur("cst")
    S.inc("bmt", 1)
    S.emit("t", lambda e, s=S.sem("bmt"), ste_n=S.cur("ste"): (e.wait_ge(S.sems["bmv"], 1),
                                           e.wait_ge(S.sems["ste"], ste_n),
                                           e.wait_ge(S.sems["cst"], CST_I128),
                                           e.transpose(PSB[:, 512:640], MRED[:, 0:128],
                                                       MRED[:, 128:256]).then_inc(s, 1)))
    for cc in range(8):
        S.inc("bhm", 1)
        def fbh(e, cc=cc, ste_n=S.cur("ste"), s=S.sem("bhm")):
            if cc == 0:
                e.wait_ge(S.sems["bmc"], 1)
                e.wait_ge(S.sems["ste"], ste_n)
            if cc >= 2:
                e.wait_ge(S.sems["bhe"], cc - 1)
            reg = PSA[:, 1024 + (cc % 2) * 512: 1536 + (cc % 2) * 512]
            e.matmul(reg, BMT[:, :], GC[:, cc * 512:(cc + 1) * 512],
                     start=True, stop=True).then_inc(s, 1)
        S.emit("t", fbh)

    for j in range(HP):
        src3 = (SLG if j % 2 == 0 else SLU)[:, :].bitcast(I32).rearrange("p (g k) -> p g k", k=32)
        S.inc("mkr", 1)
        def fmr(e, j=j, src3=src3, s=S.sem("mkr")):
            e.wait_ge(S.sems["mkl%d" % (j % 2)], 16 * (j // 2 + 1))
            if j >= 2:
                e.wait_ge(S.sems["bmm"], j - 1)
            e.reduce_sum(MRED[:, (j % 2) * 128:(j % 2) * 128 + 128].unsqueeze(2), src3,
                         axis=mybir.AxisListType.X).then_inc(s, 1)
        S.emit("v", fmr)

    emit_bm_vec()

    def fin(*specs):
        for nm, val in specs:
            S.emit("y", lambda e, nm=nm, val=val: e.wait_ge(S.sems[nm], val))

    while True:
        if stage <= 1:
            fin(("ctr", 32), ("bhw", S.cur("bhw")))
            break
        PA1 = (S.cur("mm"), S.cur("ev"), S.cur("hs"), S.cur("hv"),
               (S.cur("ha0"), S.cur("ha1")))
        work = dq_worklist()
        pos = [0]

        def pump_one():
            kind, arg = work[pos[0]]
            pos[0] += 1
            if kind in ("gu", "dn"):
                emit_dq_iter(*arg)
            elif kind == "ms":
                emit_dq_chunk_milestone()
            else:
                if dq_state["pending_fw"] is not None:
                    S.emit("g", dq_state["pending_fw"])
                    dq_state["pending_fw"] = None
                emit_wm_piece(arg)

        def gu_iters_needed(tile):
            ci = 0
            while CH_TILE_END[ci] <= tile:
                ci += 1
            return 65 * (ci + 1)

        total_work = len(work)

        for k in range(NI):
            while pos[0] < gu_iters_needed(min(k + 11, NI - 1)):
                pump_one()
            for _ in range(9):
                if pos[0] < total_work:
                    pump_one()
            extra_mm = []
            extra_ev = []
            extra_sl = []
            if k == 0:
                extra_mm = [("ctr", 32), ("ste", S.cur("ste")), ("bmv", 1), ("bmc", 1),
                            ("bhe", 8)]
                extra_ev = [("bhm", 8), ("bhw", 16 * 8)]
            if k <= 1:
                extra_sl = [("mkr", HP)]
            emit_phaseA_tile(k, 1, PA1, extra_mm, extra_ev, extra_sl)
        while pos[0] < len(work):
            pump_one()
        if dq_state["pending_fw"] is not None:
            S.emit("g", dq_state["pending_fw"])
            dq_state["pending_fw"] = None
        if stage <= 3:
            fin(("ha0", S.cur("ha0")), ("ha1", S.cur("ha1")), ("wmb", S.cur("wmb")))
            break
        emit_xc_spill(1)
        emit_phaseB(1, n_tc=phb_tc, n_grp=phb_grp)
        if stage <= 4:
            fin(("xnw", S.cur("xnw")))
            break

        xnw_all = S.cur("xnw")
        yit_all = S.cur("yit")
        yev_all = S.cur("yev")
        for hp in range(HP):
            S.inc("xrl", 16)
            def fxr(e, hp=hp, xnw_all=xnw_all, yit_all=yit_all, yev_all=yev_all, s=S.sem("xrl")):
                if hp == 0:
                    e.wait_ge(S.sems["xnw"], xnw_all)
                    e.wait_ge(S.sems["yit"], yit_all)
                    e.wait_ge(S.sems["yev"], yev_all)
                e.dma_start(out=ar_x(hp), in_=xnew_s[hp * 128:(hp + 1) * 128, :]).then_inc(s, 16)
            S.emit("g", fxr)

        emit_stats_full(2, [("xrl", S.cur("xrl"))], [("yev", S.cur("yev"))])
        emit_attn()
        if stage <= 5:
            fin(("aev", S.cur("aev")))
            break

        emit_stats_full(3, [("aev", S.cur("aev"))], [("aev", S.cur("aev"))])
        emit_xc_spill(2)

        PA2 = (S.cur("mm"), S.cur("ev"), S.cur("hs"), S.cur("hv"),
               (S.cur("ha0"), S.cur("ha1")))
        for k in range(NI):
            extra_mm = []
            extra_ev = []
            extra_sl = []
            if k == 0:
                extra_mm = [("ctr", S.cur("ctr")), ("aev", S.cur("aev")), ("ste", S.cur("ste"))]
                extra_ev = [("aev", S.cur("aev"))]
                extra_sl = [("ait", S.cur("ait"))]
            emit_phaseA_tile(k, 2, PA2, extra_mm, extra_ev, extra_sl)
        emit_phaseB(2)

        osw_all = S.cur("osw")
        S.emit("y", lambda e, n=osw_all: e.wait_ge(S.sems["osw"], n))
        break

    with nc.Block() as block:
        @block.gpsimd
        def _(eng):
            for f in S.q["g"]:
                f(eng)

        @block.sync
        def _(eng):
            for f in S.q["y"]:
                f(eng)

        @block.scalar
        def _(eng):
            for f in S.q["s"]:
                f(eng)

        @block.vector
        def _(eng):
            for f in S.q["v"]:
                f(eng)

        @block.tensor
        def _(eng):
            for f in S.q["t"]:
                f(eng)

    return nc


_NC_CACHE = [None]


def kernel(L0_attn_data, L0_attn_mask, L0_attn_weight, L0_attn_bias,
           L0_ffn_qw_gate, L0_ffn_qw_up, L0_ffn_qw_down,
           L0_ffn_s_gate, L0_ffn_z_gate, L0_ffn_s_up, L0_ffn_z_up,
           L0_ffn_s_down, L0_ffn_z_down):
    x = np.asarray(L0_attn_data, dtype=np.float32).reshape(NCORE * T, H)
    qgT = np.ascontiguousarray(np.asarray(L0_ffn_qw_gate, dtype=np.int32).T)
    quT = np.ascontiguousarray(np.asarray(L0_ffn_qw_up, dtype=np.int32).T)
    qdT = np.ascontiguousarray(np.asarray(L0_ffn_qw_down, dtype=np.int32).T)
    sgT = np.ascontiguousarray(np.asarray(L0_ffn_s_gate, dtype=np.float32).T)
    zgT = np.ascontiguousarray(np.asarray(L0_ffn_z_gate, dtype=np.float32).T)
    suT = np.ascontiguousarray(np.asarray(L0_ffn_s_up, dtype=np.float32).T)
    zuT = np.ascontiguousarray(np.asarray(L0_ffn_z_up, dtype=np.float32).T)
    sdT = np.ascontiguousarray(np.asarray(L0_ffn_s_down, dtype=np.float32).T)
    zdT = np.ascontiguousarray(np.asarray(L0_ffn_z_down, dtype=np.float32).T)
    maskA = np.ascontiguousarray(np.asarray(L0_attn_mask, dtype=np.int32))
    wT = np.ascontiguousarray(np.asarray(L0_attn_weight, dtype=np.float32).T)
    biasT = np.ascontiguousarray(np.asarray(L0_attn_bias, dtype=np.float32).reshape(32, 128).T)

    if _NC_CACHE[0] is None:
        _NC_CACHE[0] = build_nc()
    nc = _NC_CACHE[0]

    in_maps = []
    for c in range(NCORE):
        xTc = np.ascontiguousarray(x[c * T:(c + 1) * T, :].T)
        in_maps.append({
            "xT": xTc, "qgT": qgT, "quT": quT, "qdT": qdT,
            "sgT": sgT, "zgT": zgT, "suT": suT, "zuT": zuT, "sdT": sdT, "zdT": zdT,
            "mask": maskA, "wT": wT, "biasT": biasT,
        })
    res = run_bass_kernel_spmd(nc, in_maps, core_ids=list(range(NCORE)))
    outs = [np.asarray(res.results[c]["outT"]).T for c in range(NCORE)]
    return np.concatenate(outs, axis=0).reshape(4, 4096, 4096).astype(np.float32)
